# version 09
# baseline (speedup 1.0000x reference)
import numpy as np
import ml_dtypes
from contextlib import ExitStack

import concourse.bass as bass
import concourse.mybir as mybir
from concourse.bass_utils import run_bass_kernel_spmd
from concourse.bass_core_rust import IndirectOffsetOnAxis

F32 = mybir.dt.float32
BF16 = mybir.dt.bfloat16
I32 = mybir.dt.int32
AF = mybir.ActivationFunctionType
ALU = mybir.AluOpType
AX = mybir.AxisListType
BFNP = ml_dtypes.bfloat16

D = 2048
DT = D // 128
EPS_LN = 1e-5
EPS_Q = 1e-8


class Sem:
    def __init__(self, ctx, nc, name):
        self.h = ctx.enter_context(nc.semaphore(name))
        self.n = 0

    def inc(self, inst, k=1):
        inst.then_inc(self.h, k)
        self.n += k
        return inst


def build_nc(N, H):
    import os
    PHASES = int(os.environ.get("MK_PHASES", "9"))
    ATTN = int(os.environ.get("MK_ATTN", "9"))
    C = N // 512
    KT = N // 128
    Nq = N // 4
    TT_ = Nq // 128
    TC = max(1, Nq // 512)
    HT = H // 128
    Mh = 2 * C * (C + 1)
    QGRP = float(H * D)
    NG1 = (2 * HT + 31) // 32
    NG3 = (HT + 15) // 16
    NG = 2 * NG1 + NG3

    nc = bass.Bass("TRN2", target_bir_lowering=False)

    xT = nc.dram_tensor("xT", [(N // 512) * 128, DT * 512], F32, kind="ExternalInput").ap()
    x_own = nc.dram_tensor("x_own", [Nq, D], F32, kind="ExternalInput").ap()
    x_ownT = nc.dram_tensor("x_ownT", [max(1, Nq // 512) * 128, DT * 512], F32, kind="ExternalInput").ap()
    wqkv = nc.dram_tensor("wqkv", [D, 2 * 768], F32, kind="ExternalInput").ap()
    wo = nc.dram_tensor("wo", [D, D], F32, kind="ExternalInput").ap()
    w1s = nc.dram_tensor("w1s", [2 * HT, 128 * 128], F32, kind="ExternalInput").ap()
    w2s = nc.dram_tensor("w2s", [2 * HT, 128 * 128], F32, kind="ExternalInput").ap()
    w3s = nc.dram_tensor("w3s", [HT, 128 * 256], F32, kind="ExternalInput").ap()
    cosin = nc.dram_tensor("cosin", [128, N], F32, kind="ExternalInput").ap()
    cosin2 = nc.dram_tensor("cosin2", [128, N], F32, kind="ExternalInput").ap()
    masksd = nc.dram_tensor("masksd", [128, 4 * 512], BF16, kind="ExternalInput").ap()
    ones2 = nc.dram_tensor("ones2", [128, 128], F32, kind="ExternalInput").ap()
    iden = nc.dram_tensor("iden", [128, 128], F32, kind="ExternalInput").ap()
    yidx = nc.dram_tensor("yidx", [128, 4], I32, kind="ExternalInput").ap()

    out = nc.dram_tensor("out", [Nq, D], F32, kind="ExternalOutput").ap()

    ar_in = nc.dram_tensor("ar_in", [1, 8], F32)
    ar_out = nc.dram_tensor("ar_out", [1, 8], F32)
    w1qi = nc.dram_tensor("w1qi", [2 * HT, 128 * 128], BF16)
    w1qo = nc.dram_tensor("w1qo", [16 * HT, 128 * 128], BF16)
    w2qi = nc.dram_tensor("w2qi", [2 * HT, 128 * 128], BF16)
    w2qo = nc.dram_tensor("w2qo", [16 * HT, 128 * 128], BF16)
    w3qi = nc.dram_tensor("w3qi", [HT, 128 * 256], BF16)
    w3qo = nc.dram_tensor("w3qo", [8 * HT, 128 * 256], BF16)
    yin = [nc.dram_tensor(f"yin{i}", [128, N], BF16) for i in range(4)]
    yout = [nc.dram_tensor(f"yout{i}", [512, N], BF16) for i in range(4)]

    RG_ALL = [list(range(8))]
    RG_HALF = [[0, 1, 2, 3], [4, 5, 6, 7]]

    ctx = ExitStack()
    with ctx:
        S_ = lambda name: Sem(ctx, nc, name)
        sCONST = S_("sCONST"); sWQ = S_("sWQ"); sCM = S_("sCM"); sVN1 = S_("sVN1")
        sGW = S_("sGW"); sGWg = [S_(f"sGWg{i}") for i in range(4)]
        sQO = [S_(f"sQO{i}") for i in range(4)]
        sGR = S_("sGR"); sGMM = S_("sGMM"); sGEV = S_("sGEV")
        sGDMA = S_("sGDMA"); sARC = S_("sARC"); sSCV = S_("sSCV"); sQV = S_("sQV")
        sAGW = [S_(f"sAGW{i}") for i in range(3)]
        sXT = [S_(f"sXT{i}") for i in range(2)]
        sSQ = S_("sSQ"); sSQ2 = S_("sSQ2"); sSTAT = S_("sSTAT")
        sROW1 = S_("sROW1"); sROWS = S_("sROWS"); sROW2 = S_("sROW2")
        sBC = S_("sBC"); sBCE = S_("sBCE"); sRT = S_("sRT"); sRTE = S_("sRTE")
        sWS = S_("sWS"); sWST = S_("sWST")
        sQK = S_("sQK"); sPRE = S_("sPRE"); sROPE = S_("sROPE"); sCSR = S_("sCSR")
        sVM = S_("sVM"); sVEV = S_("sVEV")
        sST = S_("sST"); sEXP = S_("sEXP"); sMSK = S_("sMSK"); sPVT = S_("sPVT")
        sREC = S_("sREC"); sYEV = S_("sYEV"); sYTR = S_("sYTR"); sYTS = S_("sYTS")
        sYD = S_("sYD")
        sAGY = [S_(f"sAGY{i}") for i in range(4)]
        sYO = [S_(f"sYO{i}") for i in range(4)]
        sWO = [S_(f"sWO{i}") for i in range(4)]
        sWOT = S_("sWOT"); sSEV = S_("sSEV")
        sXO = S_("sXO"); sFSQ = S_("sFSQ"); sFST = S_("sFST"); sFR1 = S_("sFR1")
        sFRS = S_("sFRS"); sFBC = S_("sFBC"); sFBE = S_("sFBE"); sXN = S_("sXN")
        sW12 = [S_(f"sW12_{i}") for i in range(2)]
        sFM = S_("sFM"); sFE = S_("sFE"); sHID = S_("sHID")
        sW3L = [S_(f"sW3L{i}") for i in range(4)]
        sW3T = S_("sW3T"); sGE = S_("sGE"); sSG = S_("sSG")
        sXL = [S_(f"sXL{i}") for i in range(2)]
        sVF = S_("sVF"); sOUT = [S_(f"sOUT{i}") for i in range(2)]
        sVT = S_("sVT"); sSS = S_("sSS")

        idf = ctx.enter_context(nc.sbuf_tensor("idf", [128, 128], F32))
        idb = ctx.enter_context(nc.sbuf_tensor("idb", [128, 128], BF16))
        onesf = ctx.enter_context(nc.sbuf_tensor("onesf", [128, 128], F32))
        onesb = ctx.enter_context(nc.sbuf_tensor("onesb", [128, 128], BF16))
        rstd_col = ctx.enter_context(nc.sbuf_tensor("rstd_col", [128, KT], F32))
        scvec = ctx.enter_context(nc.sbuf_tensor("scvec", [128, 4], F32))
        garow = ctx.enter_context(nc.sbuf_tensor("garow", [1, 8], F32))
        gsrow = ctx.enter_context(nc.sbuf_tensor("gsrow", [1, 8], F32))
        gtmp2 = ctx.enter_context(nc.sbuf_tensor("gtmp2", [1, 4], F32))
        gtmp3 = ctx.enter_context(nc.sbuf_tensor("gtmp3", [1, 4], F32))
        gscal = ctx.enter_context(nc.sbuf_tensor("gscal", [1, 4], F32))
        gacc = ctx.enter_context(nc.sbuf_tensor("gacc", [128, 8], F32))
        gaccF = ctx.enter_context(nc.sbuf_tensor("gaccF", [128, 4], F32))
        gtmp = ctx.enter_context(nc.sbuf_tensor("gtmp", [128, 1], F32))
        yix = ctx.enter_context(nc.sbuf_tensor("yix", [128, 4], I32))

        gam_specs = [
            (w1s, 2 * HT, 128, 32, w1qi, w1qo),
            (w2s, 2 * HT, 128, 32, w2qi, w2qo),
            (w3s, HT, 256, 16, w3qi, w3qo),
        ]
        groups = []
        for si, (din, rows, relC, grows, dq, dout) in enumerate(gam_specs):
            for r0 in range(0, rows, grows):
                groups.append((si, r0, min(grows, rows - r0)))
        assert len(groups) == NG
        gcnt = [sum(1 for g in groups if g[0] == si) for si in range(3)]

        with ExitStack() as pctx:
            gbuf = pctx.enter_context(nc.sbuf_tensor("gbuf", [128, 4 * 4096], F32))
            qbuf = pctx.enter_context(nc.sbuf_tensor("qbuf", [128, 4 * 4096], BF16))
            qa = pctx.enter_context(nc.sbuf_tensor("qa", [128, 4096], F32))
            qb = pctx.enter_context(nc.sbuf_tensor("qb", [128, 4096], F32))
            ps_g = pctx.enter_context(nc.psum_tensor("ps_g", [128, 512], F32))
            blkA = pctx.enter_context(nc.Block())

            @blkA.gpsimd
            def _(g):
                sCONST.inc(g.dma_start(idb[:], iden), 16)
                sCONST.inc(g.dma_start(onesb[:], ones2), 16)
                sCONST.inc(g.dma_start(yix[:], yidx), 16)
                g.wait_ge(sGDMA.h, 16)
                sARC.inc(g.collective_compute(
                    "AllReduce", ALU.add, replica_groups=RG_ALL,
                    ins=[ar_in.ap().opt()], outs=[ar_out.ap().opt()]))
                for qi_ in range(4):
                    g.wait_ge(sQO[qi_].h, 16 * ((NG - qi_ + 3) // 4))

            @blkA.sync
            def _(s):
                sGW.inc(s.dma_start(idf[:], iden), 16)
                sGW.inc(s.dma_start(onesf[:], ones2), 16)
                for gi, (si, r0, nr) in enumerate(groups):
                    din, rows, relC, grows, dq, dout = gam_specs[si]
                    if gi >= 4:
                        s.wait_ge(sGR.h, gi - 3)
                    sGWg[gi % 4].inc(s.dma_start(
                        gbuf[:, (gi % 4) * 4096:(gi % 4) * 4096 + nr * relC],
                        din.rearrange("r (p c) -> p r c", p=128)[:, r0:r0 + nr, :]),
                        16)
                s.wait_ge(sGEV.h, 1)
                sGDMA.inc(s.dma_start(ar_in.ap(), garow[:]), 16)
                s.wait_ge(sARC.h, 1)
                sGDMA.inc(s.dma_start(gsrow[:], ar_out.ap()), 16)
                for gi, (si, r0, nr) in enumerate(groups):
                    din, rows, relC, grows, dq, dout = gam_specs[si]
                    if gi < 4:
                        s.wait_ge(sGR.h, NG)
                    else:
                        s.wait_ge(sQV.h, gi - 3)
                    sGWg[(NG + gi) % 4].inc(s.dma_start(
                        gbuf[:, (gi % 4) * 4096:(gi % 4) * 4096 + nr * relC],
                        din.rearrange("r (p c) -> p r c", p=128)[:, r0:r0 + nr, :]),
                        16)
                    s.wait_ge(sQV.h, gi + 1)
                    sQO[gi % 4].inc(s.dma_start(
                        dq.rearrange("r (p c) -> p r c", p=128)[:, r0:r0 + nr, :],
                        qbuf[:, (gi % 4) * 4096:(gi % 4) * 4096 + nr * relC]), 16)

            @blkA.tensor
            def _(t):
                t.wait_ge(sGW.h, 32)
                t.wait_ge(sVT.h, NG + 3)
                sGMM.inc(t.matmul(ps_g[0:1, 0:3], onesf[:, 0:1], gaccF[:, 0:3],
                                  start=True, stop=True))
                t.wait_ge(sSCV.h, 1)
                t.wait_ge(sGEV.h, 1)
                sGMM.inc(t.matmul(ps_g[:, 0:3], onesf[0:1, :], gscal[0:1, 0:3],
                                  start=True, stop=True))

            @blkA.scalar
            def _(sc):
                sSS.inc(sc.memzero(garow[0:1, 0:8]))
                sc.wait_ge(sGMM.h, 1)
                sc.wait_ge(sSS.h, sSS.n)
                sGEV.inc(sc.activation(garow[0:1, 0:3], ps_g[0:1, 0:3], AF.Copy))
                sc.wait_ge(sGMM.h, 2)
                sGEV.inc(sc.activation(scvec[:, 0:3], ps_g[:, 0:3], AF.Copy))

            @blkA.vector
            def _(v):
                cnt = [0, 0, 0]
                for gi, (si, r0, nr) in enumerate(groups):
                    relC = gam_specs[si][2]
                    v.wait_ge(sGWg[gi % 4].h, 16 * (gi // 4 + 1))
                    if gi >= 1:
                        v.wait_ge(sVT.h, sVT.n)
                    i1 = v.tensor_reduce(
                        gtmp[:], gbuf[:, (gi % 4) * 4096:(gi % 4) * 4096 + nr * relC],
                        AX.X, ALU.add, apply_absolute_value=True)
                    sGR.inc(i1)
                    v.wait_ge(sGR.h, sGR.n)
                    if cnt[si] == 0:
                        sVT.inc(v.tensor_copy(gacc[:, si * 2:si * 2 + 1], gtmp[:]))
                    else:
                        psrc = si * 2 + (cnt[si] - 1) % 2
                        pdst = si * 2 + cnt[si] % 2
                        sVT.inc(v.tensor_add(gacc[:, pdst:pdst + 1],
                                             gacc[:, psrc:psrc + 1], gtmp[:]))
                    cnt[si] += 1
                for si in range(3):
                    pfin = si * 2 + (cnt[si] - 1) % 2
                    v.wait_ge(sVT.h, sVT.n)
                    sVT.inc(v.tensor_copy(gaccF[:, si:si + 1],
                                          gacc[:, pfin:pfin + 1]))
                v.wait_ge(sGDMA.h, 32)
                sVT.inc(v.tensor_scalar_mul(gtmp2[0:1, 0:3], gsrow[0:1, 0:3],
                                            1.0 / QGRP))
                v.wait_ge(sVT.h, sVT.n)
                sVT.inc(v.tensor_scalar_add(gtmp3[0:1, 0:3], gtmp2[0:1, 0:3], EPS_Q))
                v.wait_ge(sVT.h, sVT.n)
                sSCV.inc(v.reciprocal(gscal[0:1, 0:3], gtmp3[0:1, 0:3]))
                v.wait_ge(sGEV.h, 2)
                for gi, (si, r0, nr) in enumerate(groups):
                    relC = gam_specs[si][2]
                    w = nr * relC
                    v.wait_ge(sGWg[(NG + gi) % 4].h, 16 * ((NG + gi) // 4 + 1))
                    if gi >= 4:
                        v.wait_ge(sQO[gi % 4].h, 16 * (gi // 4))
                    if gi >= 1:
                        v.wait_ge(sQV.h, gi)
                    src = gbuf[:, (gi % 4) * 4096:(gi % 4) * 4096 + w]
                    i1 = v.tensor_scalar(qa[:, 0:w], src, scvec[:, si:si + 1], 0.5,
                                         ALU.mult, ALU.is_ge)
                    sVT.inc(i1)
                    i2 = v.tensor_scalar(qb[:, 0:w], src, scvec[:, si:si + 1], -0.5,
                                         ALU.mult, ALU.is_le)
                    sVT.inc(i2)
                    v.wait_ge(sVT.h, sVT.n)
                    i3 = v.tensor_sub(qbuf[:, (gi % 4) * 4096:(gi % 4) * 4096 + w],
                                      qa[:, 0:w], qb[:, 0:w])
                    sQV.inc(i3)

        with ExitStack() as actx:
            wqkv_sb = actx.enter_context(nc.sbuf_tensor("wqkv_sb", [128, DT * 768], BF16))
            cs_sb = actx.enter_context(nc.sbuf_tensor("cs_sb", [128, N], BF16))
            cs2_sb = actx.enter_context(nc.sbuf_tensor("cs2_sb", [128, N], BF16))
            QTs = actx.enter_context(nc.sbuf_tensor("QTs", [128, 2 * N], BF16))
            KTs = actx.enter_context(nc.sbuf_tensor("KTs", [128, 2 * N], BF16))
            Vn = actx.enter_context(nc.sbuf_tensor("Vn", [128, 2 * KT * 132], BF16))
            xTb = actx.enter_context(nc.sbuf_tensor("xTb", [128, 2 * DT * 512], BF16))
            xsqb = actx.enter_context(nc.sbuf_tensor("xsqb", [128, (DT // 2) * 512], BF16))
            masks = actx.enter_context(nc.sbuf_tensor("masks", [128, 4 * 512], BF16))
            prer_e = actx.enter_context(nc.sbuf_tensor("prer_e", [128, 2 * 512], BF16))
            prer_o = actx.enter_context(nc.sbuf_tensor("prer_o", [128, 2 * 512], BF16))
            rt1 = actx.enter_context(nc.sbuf_tensor("rt1", [128, 512], BF16))
            rt2 = actx.enter_context(nc.sbuf_tensor("rt2", [128, 512], BF16))
            rbc = actx.enter_context(nc.sbuf_tensor("rbc", [128, 2 * 512], BF16))
            cosr = actx.enter_context(nc.sbuf_tensor("cosr", [128, 512], BF16))
            sinr = actx.enter_context(nc.sbuf_tensor("sinr", [128, 512], BF16))
            Ebuf = actx.enter_context(nc.sbuf_tensor("Ebuf", [128, 3 * 512], BF16))
            EbufM = actx.enter_context(nc.sbuf_tensor("EbufM", [128, 2 * 512], BF16))
            ynat = actx.enter_context(nc.sbuf_tensor("ynat", [128, 4 * 128], BF16))
            rec = actx.enter_context(nc.sbuf_tensor("rec", [128, 4], F32))
            yT_send = actx.enter_context(nc.sbuf_tensor("yT_send", [128, N], BF16))
            m_row = actx.enter_context(nc.sbuf_tensor("m_row", [1, N], BF16))
            rstd_row = actx.enter_context(nc.sbuf_tensor("rstd_row", [1, N], BF16))
            ws_neg = actx.enter_context(nc.sbuf_tensor("ws_neg", [1, 768], BF16))
            m_f32 = actx.enter_context(nc.sbuf_tensor("m_f32", [1, 512], F32))
            s2f = actx.enter_context(nc.sbuf_tensor("s2f", [1, 512], F32))
            m2f = actx.enter_context(nc.sbuf_tensor("m2f", [1, 512], F32))
            varf = actx.enter_context(nc.sbuf_tensor("varf", [1, 512], F32))
            lnvf = actx.enter_context(nc.sbuf_tensor("lnvf", [1, 512], F32))
            rstdf = actx.enter_context(nc.sbuf_tensor("rstdf", [1, 512], F32))

            def emit_pass(p):
                with ExitStack() as qctx:
                    ps_stats = qctx.enter_context(nc.psum_tensor(f"ps_stats{p}", [128, 512], F32))
                    ps_bc = qctx.enter_context(nc.psum_tensor(f"ps_bc{p}", [128, 512], F32))
                    ps_rt = qctx.enter_context(nc.psum_tensor(f"ps_rt{p}", [128, 128], BF16))
                    ps_yt1 = qctx.enter_context(nc.psum_tensor(f"ps_yt1{p}", [128, 128], BF16))
                    ps_qk = [qctx.enter_context(nc.psum_tensor(f"ps_qk{p}_{i}", [128, 512], F32)) for i in range(2)]
                    ps_v = [qctx.enter_context(nc.psum_tensor(f"ps_v{p}_{i}", [128, 512], F32)) for i in range(2)]
                    blk = qctx.enter_context(nc.Block())

                    @blk.gpsimd
                    def _(g):
                        if p == 0:
                            sWQ.inc(g.dma_start(
                                wqkv_sb[:],
                                wqkv.rearrange("(dt p) f -> p dt f", p=128)[:, :, 0:768]),
                                16)
                            sCM.inc(g.dma_start(cs_sb[:], cosin), 16)
                            sCM.inc(g.dma_start(cs2_sb[:], cosin2), 16)
                            sCM.inc(g.dma_start(masks[:], masksd), 16)
                        if p == 1:
                            sWQ.inc(g.dma_start(
                                wqkv_sb[:],
                                wqkv.rearrange("(dt p) f -> p dt f", p=128)[:, :, 768:1536]),
                                16)
                            for qi_ in range(4):
                                g.wait_ge(sQO[qi_].h, 16 * ((NG - qi_ + 3) // 4))
                            for si in range(2):
                                sAGW[si].inc(g.collective_compute(
                                    "AllGather", ALU.bypass, replica_groups=RG_ALL,
                                    ins=[gam_specs[si][4].ap().opt()],
                                    outs=[gam_specs[si][5].ap().opt()]))
                        for c in range(C):
                            k = C * p + c
                            if k >= 2:
                                g.wait_ge(sVM.h, 4 * (k - 1))
                                if p == 0 and c >= 2:
                                    g.wait_ge(sSQ.h, 2 * (c - 1))
                            sXT[k % 2].inc(g.dma_start(
                                xTb[:, (c % 2) * DT * 512:(c % 2 + 1) * DT * 512],
                                xT[c * 128:(c + 1) * 128, :]),
                                16)
                        if ATTN < 5:
                            return
                        for hl in range(2):
                            hg = 2 * p + hl
                            g.wait_ge(sYD.h, 16 * (hg + 1))
                            sAGY[hg].inc(g.collective_compute(
                                "AllGather", ALU.bypass, replica_groups=RG_HALF,
                                ins=[yin[hg].ap().opt()], outs=[yout[hg].ap().opt()]))

                    @blk.tensor
                    def _(t):
                        t.wait_ge(sCONST.h, 48)
                        t.wait_ge(sWQ.h, 16 * (p + 1))
                        for sp in range(3):
                            if p + sp > 0:
                                t.wait_ge(sWS.h, 3 * p + sp)
                            for dt in range(DT):
                                ii = t.matmul(ps_bc[0:1, 0:256], onesb[:, 0:1],
                                              wqkv_sb[:, dt * 768 + sp * 256:dt * 768 + (sp + 1) * 256],
                                              start=(dt == 0), stop=(dt == DT - 1))
                            sWST.inc(ii)
                        t.wait_ge(sWS.h, 3 * (p + 1))

                        def emit_stats(c):
                            k = C * p + c
                            t.wait_ge(sXT[k % 2].h, 16 * (k // 2 + 1))
                            if c >= 1:
                                t.wait_ge(sROW1.h, c)
                            for dt in range(DT):
                                t.matmul(ps_stats[0:1, 0:512], onesb[:, 0:1],
                                         xTb[:, (c % 2) * DT * 512 + dt * 512:(c % 2) * DT * 512 + (dt + 1) * 512],
                                         start=(dt == 0), stop=(dt == DT - 1))
                            for half in range(2):
                                t.wait_ge(sSQ.h, 2 * c + half + 1)
                                for dt in range(DT // 2):
                                    ii = t.matmul(ps_stats[64:65, 0:512], onesb[:, 0:1],
                                                  xsqb[:, dt * 512:(dt + 1) * 512],
                                                  start=(half == 0 and dt == 0),
                                                  stop=(half == 1 and dt == DT // 2 - 1))
                                if half == 0:
                                    sSQ2.inc(ii)
                            sSTAT.inc(ii)

                        if p == 0:
                            emit_stats(0)
                        for c in range(C):
                            k = C * p + c
                            t.wait_ge(sXT[k % 2].h, 16 * (k // 2 + 1))
                            if p == 0:
                                t.wait_ge(sROW2.h, 2 * (c + 1))
                            t.wait_ge(sBCE.h, C * p + c)
                            sBC.inc(t.matmul(ps_bc[:, 0:512], onesb[0:1, :],
                                             rstd_row[0:1, c * 512:(c + 1) * 512],
                                             start=True, stop=True))
                            if p == 0:
                                for j in range(4):
                                    if j == 0:
                                        t.wait_ge(sBCE.h, c + 1)
                                    t.wait_ge(sRTE.h, 4 * c + j)
                                    sRT.inc(t.transpose(
                                        ps_rt[:],
                                        rbc[:, (c % 2) * 512 + j * 128:(c % 2) * 512 + (j + 1) * 128],
                                        idb[:]))
                                if c + 1 < C:
                                    emit_stats(c + 1)
                            for ft in range(4):
                                gq = 4 * k + ft
                                if gq >= 2:
                                    t.wait_ge(sPRE.h, gq - 1)
                                for dt in range(DT):
                                    t.matmul(ps_qk[gq % 2][:],
                                             wqkv_sb[:, dt * 768 + ft * 128:dt * 768 + (ft + 1) * 128],
                                             xTb[:, (c % 2) * DT * 512 + dt * 512:(c % 2) * DT * 512 + (dt + 1) * 512],
                                             start=(dt == 0), stop=False)
                                off = ft * 128 if ft < 2 else 256 + (ft - 2) * 128
                                sQK.inc(t.matmul(ps_qk[gq % 2][:],
                                                 ws_neg[0:1, off:off + 128],
                                                 m_row[0:1, c * 512:(c + 1) * 512],
                                                 start=False, stop=True))
                            for kt in range(4):
                                gv = 4 * k + kt
                                if gv >= 2:
                                    t.wait_ge(sVEV.h, gv - 1)
                                for dt in range(DT):
                                    t.matmul(ps_v[gv % 2][:, 0:256],
                                             xTb[:, (c % 2) * DT * 512 + dt * 512 + kt * 128:(c % 2) * DT * 512 + dt * 512 + (kt + 1) * 128],
                                             wqkv_sb[:, dt * 768 + 512:dt * 768 + 768],
                                             start=(dt == 0), stop=False)
                                sVM.inc(t.matmul(ps_v[gv % 2][:, 0:256],
                                                 m_row[0:1, c * 512 + kt * 128:c * 512 + (kt + 1) * 128],
                                                 ws_neg[0:1, 512:768],
                                                 start=False, stop=True))
                        t.wait_ge(sROPE.h, 4 * C * (p + 1))
                        t.wait_ge(sVEV.h, 4 * C * (p + 1))
                        t.wait_ge(sVN1.h, 1)
                        if p == 0:
                            t.wait_ge(sRTE.h, 4 * C)
                        t.wait_ge(sBCE.h, C * (p + 1))

                    ps_st = ps_qk
                    ps_pv = [ps_v[0], ps_v[1], ps_stats, ps_bc]
                    ps_yt = [ps_rt, ps_yt1]

                    def d_decomp(dg):
                        hgp = dg // (4 * C)
                        r = dg % (4 * C)
                        return hgp, r // 4, r % 4

                    def tile_idx(hg, qc, kb):
                        return hg * Mh + 2 * qc * (qc + 1) + kb

                    @blk.tensor
                    def _(t):
                        if ATTN == 0:
                            return
                        for hl in range(2):
                            hg = 2 * p + hl

                            def emit_ytr(qcq):
                                for j in range(4):
                                    g4 = hg * 4 * C + 4 * qcq + j
                                    t.wait_ge(sYEV.h, g4 + 1)
                                    if ATTN >= 5 and g4 >= 2:
                                        t.wait_ge(sYTS.h, g4 - 1)
                                    sYTR.inc(t.transpose(
                                        ps_yt[g4 % 2][:],
                                        ynat[:, j * 128:(j + 1) * 128], idb[:]))

                            def emit_pv(qc, kb):
                                i = tile_idx(hg, qc, kb)
                                gq = hg * C + qc
                                diag = kb >= 4 * qc
                                if diag:
                                    dg = hg * 4 * C + 4 * qc + (kb - 4 * qc)
                                    t.wait_ge(sMSK.h, dg + 1)
                                    src = EbufM[:, (dg % 2) * 512:(dg % 2 + 1) * 512]
                                else:
                                    t.wait_ge(sEXP.h, i + 1)
                                    src = Ebuf[:, (i % 3) * 512:(i % 3 + 1) * 512]
                                if kb == 0 and gq >= 2:
                                    t.wait_ge(sYEV.h, 4 * (gq - 1))
                                for jq in range(4):
                                    pv = ps_pv[(qc % 2) * 2 + jq // 2]
                                    ii = t.matmul(
                                        pv[:, (jq % 2) * 130:(jq % 2) * 130 + 129],
                                        src[:, jq * 128:(jq + 1) * 128],
                                        Vn[:, hl * KT * 132 + kb * 132:hl * KT * 132 + kb * 132 + 129],
                                        start=(kb == 0 and jq % 2 == 0),
                                        stop=(kb == 4 * qc + 3 and jq % 2 == 1))
                                    if jq == 3:
                                        sPVT.inc(ii)

                            pend = None
                            for qc in range(C):
                                for kb in range(4 * qc + 4):
                                    i = tile_idx(hg, qc, kb)
                                    if i >= 2:
                                        t.wait_ge(sEXP.h, i - 1)
                                    sST.inc(t.matmul(
                                        ps_st[i % 2][:],
                                        KTs[:, hl * N + kb * 128:hl * N + (kb + 1) * 128],
                                        QTs[:, hl * N + qc * 512:hl * N + (qc + 1) * 512],
                                        start=True, stop=True))
                                    if ATTN >= 3 and pend is not None:
                                        emit_pv(*pend)
                                        if pend[1] == 4 * pend[0] + 3:
                                            if ATTN >= 4 and pend[0] >= 1:
                                                emit_ytr(pend[0] - 1)
                                    pend = (qc, kb)
                            if ATTN >= 3 and pend is not None:
                                emit_pv(*pend)
                                if ATTN >= 4:
                                    if C >= 2:
                                        emit_ytr(C - 2)
                                    emit_ytr(C - 1)

                    @blk.scalar
                    def _(sc):
                        for sp in range(3):
                            sc.wait_ge(sWST.h, 3 * p + sp + 1)
                            sWS.inc(sc.activation(ws_neg[0:1, sp * 256:(sp + 1) * 256],
                                                  ps_bc[0:1, 0:256], AF.Copy, scale=-1.0))
                        def emit_squares(c, half):
                            if half == 0:
                                sc.wait_ge(sXT[c % 2].h, 16 * (c // 2 + 1))
                                if c >= 1:
                                    sc.wait_ge(sSTAT.h, c)
                            else:
                                sc.wait_ge(sSQ2.h, c + 1)
                            for dt in range(DT // 2):
                                dg = half * (DT // 2) + dt
                                ii = sc.activation(
                                    xsqb[:, dt * 512:(dt + 1) * 512],
                                    xTb[:, (c % 2) * DT * 512 + dg * 512:(c % 2) * DT * 512 + (dg + 1) * 512],
                                    AF.Square)
                            sSQ.inc(ii)

                        def srows(j):
                            sc.wait_ge(sROW1.h, j + 1)
                            if j >= 1:
                                sc.wait_ge(sROW2.h, 2 * j)
                            sSS.inc(sc.activation(varf[0:1, :], lnvf[0:1, :], AF.Ln))
                            sc.wait_ge(sSS.h, sSS.n)
                            sROWS.inc(sc.activation(rstdf[0:1, :], varf[0:1, :],
                                                    AF.Exp, scale=-0.5))

                        if p == 0:
                            emit_squares(0, 0)
                            emit_squares(0, 1)
                            srows(0)
                        for c in range(C):
                            k = C * p + c
                            sc.wait_ge(sBC.h, C * p + c + 1)
                            if C * p + c >= 2:
                                sc.wait_ge(sCSR.h, C * p + c - 1)
                                if p == 0 and c >= 2:
                                    sc.wait_ge(sRT.h, 4 * (c - 1))
                            sBCE.inc(sc.activation(
                                rbc[:, (c % 2) * 512:(c % 2 + 1) * 512],
                                ps_bc[:, 0:512], AF.Copy))
                            if p == 0:
                                for j in range(4):
                                    sc.wait_ge(sRT.h, 4 * c + j + 1)
                                    sRTE.inc(sc.activation(
                                        rstd_col[:, 4 * c + j:4 * c + j + 1],
                                        ps_rt[:, 0:1], AF.Copy))
                                if c + 1 < C:
                                    emit_squares(c + 1, 0)
                                    emit_squares(c + 1, 1)
                                    srows(c + 1)
                            for ft in range(4):
                                gq = 4 * k + ft
                                sc.wait_ge(sQK.h, gq + 1)
                                if gq >= 2:
                                    sc.wait_ge(sROPE.h, gq - 1)
                                sl = slice((gq % 2) * 512, (gq % 2 + 1) * 512)
                                sc.activation(prer_e[0:64, sl], ps_qk[gq % 2][0:64, :], AF.Copy)
                                sc.activation(prer_e[64:128, sl], ps_qk[gq % 2][0:64, :], AF.Copy)
                                sc.activation(prer_o[0:64, sl], ps_qk[gq % 2][64:128, :], AF.Copy)
                                sPRE.inc(sc.activation(prer_o[64:128, sl],
                                                       ps_qk[gq % 2][64:128, :], AF.Copy))
                            for kt in range(4):
                                gv = 4 * k + kt
                                sc.wait_ge(sVM.h, gv + 1)
                                ktg = 4 * c + kt
                                if p == 0:
                                    sc.wait_ge(sRTE.h, ktg + 1)
                                vview = Vn.ap().rearrange(
                                    "p (h k w) -> p h k w", h=2, k=KT)[:, :, ktg, 0:128]
                                sVEV.inc(sc.activation(
                                    vview, ps_v[gv % 2][:, 0:256], AF.Copy,
                                    scale=rstd_col[:, ktg:ktg + 1]))
                        if ATTN == 0:
                            return
                        for hl in range(2):
                            hg = 2 * p + hl

                            def emit_yts1(qcq, jq):
                                g4 = hg * 4 * C + 4 * qcq + jq
                                if ATTN >= 5 and qcq == 0 and jq == 0:
                                    sc.wait_ge(sYD.h, 16 * hg)
                                sc.wait_ge(sYTR.h, g4 + 1)
                                sYTS.inc(sc.activation(
                                    yT_send[:, (qcq * 4 + jq) * 128:(qcq * 4 + jq + 1) * 128],
                                    ps_yt[g4 % 2][:],
                                    AF.Copy))

                            for qc in range(C):
                                gq = hg * C + qc
                                for kb in range(4 * qc + 4):
                                    i = tile_idx(hg, qc, kb)
                                    sc.wait_ge(sST.h, i + 1)
                                    if ATTN >= 3 and i >= 3:
                                        sc.wait_ge(sPVT.h, i - 2)
                                    sEXP.inc(sc.activation(
                                        Ebuf[:, (i % 3) * 512:(i % 3 + 1) * 512],
                                        ps_st[i % 2][:], AF.Exp))
                                for jq in range(4):
                                    if ATTN < 3:
                                        continue
                                    g4 = hg * 4 * C + 4 * qc + jq
                                    sc.wait_ge(sREC.h, g4 + 1)
                                    if ATTN >= 4 and g4 >= 4:
                                        sc.wait_ge(sYTR.h, g4 - 3)
                                    pv = ps_pv[(qc % 2) * 2 + jq // 2]
                                    sYEV.inc(sc.activation(
                                        ynat[:, jq * 128:(jq + 1) * 128],
                                        pv[:, (jq % 2) * 130:(jq % 2) * 130 + 128],
                                        AF.Copy, scale=rec[:, jq:jq + 1]))
                                    if ATTN >= 5 and qc >= 1:
                                        emit_yts1(qc - 1, jq)
                            if ATTN >= 5:
                                for jq in range(4):
                                    emit_yts1(C - 1, jq)

                    @blk.vector
                    def _(v):
                        v.wait_ge(sCM.h, 48)
                        def vrows(j):
                            v.wait_ge(sSTAT.h, j + 1)
                            if j >= 1:
                                v.wait_ge(sROW2.h, 2 * j)
                                v.wait_ge(sROWS.h, j)
                            sVT.inc(v.tensor_scalar_mul(m_f32[0:1, :],
                                                        ps_stats[0:1, 0:512], 1.0 / D))
                            sVT.inc(v.tensor_scalar_mul(s2f[0:1, :],
                                                        ps_stats[64:65, 0:512], 1.0 / D))
                            v.wait_ge(sVT.h, sVT.n)
                            sVT.inc(v.tensor_mul(m2f[0:1, :], m_f32[0:1, :],
                                                 m_f32[0:1, :]))
                            v.wait_ge(sVT.h, sVT.n)
                            sVT.inc(v.tensor_sub(varf[0:1, :], s2f[0:1, :],
                                                 m2f[0:1, :]))
                            v.wait_ge(sVT.h, sVT.n)
                            sROW1.inc(v.tensor_scalar_add(lnvf[0:1, :],
                                                          varf[0:1, :], EPS_LN))
                            v.wait_ge(sROWS.h, j + 1)
                            sROW2.inc(v.tensor_copy(m_row[0:1, j * 512:(j + 1) * 512],
                                                    m_f32[0:1, :]))
                            sROW2.inc(v.tensor_copy(
                                rstd_row[0:1, j * 512:(j + 1) * 512], rstdf[0:1, :]))

                        if p == 0:
                            sVN1.inc(v.memset(Vn.ap().rearrange(
                                "p (h k w) -> p h k w", h=2, k=KT)[:, :, :, 128:129], 1.0))
                            vrows(0)
                        for c in range(C):
                            k = C * p + c
                            v.wait_ge(sBCE.h, C * p + c + 1)
                            if k >= 1:
                                v.wait_ge(sROPE.h, 4 * k)
                            i1 = v.tensor_mul(cosr[:],
                                              cs_sb[:, c * 512:(c + 1) * 512],
                                              rbc[:, (c % 2) * 512:(c % 2 + 1) * 512])
                            sVT.inc(i1)
                            csr_done = sVT.n
                            i2 = v.tensor_mul(sinr[:],
                                              cs2_sb[:, c * 512:(c + 1) * 512],
                                              rbc[:, (c % 2) * 512:(c % 2 + 1) * 512])
                            sCSR.inc(i2)
                            if p == 0 and c + 1 < C:
                                vrows(c + 1)
                            for ft in range(4):
                                gq = 4 * k + ft
                                v.wait_ge(sPRE.h, gq + 1)
                                v.wait_ge(sVT.h, csr_done)
                                v.wait_ge(sCSR.h, C * p + c + 1)
                                sl = slice((gq % 2) * 512, (gq % 2 + 1) * 512)
                                if ft < 2:
                                    dst = QTs[:, ft * N + c * 512:ft * N + (c + 1) * 512]
                                else:
                                    dst = KTs[:, (ft - 2) * N + c * 512:(ft - 2) * N + (c + 1) * 512]
                                if sROPE.n > 0:
                                    v.wait_ge(sROPE.h, sROPE.n)
                                sVT.inc(v.tensor_mul(rt1[:], prer_e[:, sl], cosr[:]))
                                sVT.inc(v.tensor_mul(rt2[:], prer_o[:, sl], sinr[:]))
                                v.wait_ge(sVT.h, sVT.n)
                                sROPE.inc(v.tensor_add(dst, rt1[:], rt2[:]))
                        if ATTN < 3:
                            return
                        for hl in range(2):
                            hg = 2 * p + hl
                            for qc in range(C):
                                for j in range(4):
                                    kb = 4 * qc + j
                                    i = tile_idx(hg, qc, kb)
                                    dg = hg * 4 * C + 4 * qc + j
                                    v.wait_ge(sEXP.h, i + 1)
                                    if dg >= 2:
                                        hgp, qcp, jp = d_decomp(dg - 2)
                                        v.wait_ge(sPVT.h,
                                                  tile_idx(hgp, qcp, 4 * qcp + jp) + 1)
                                    sMSK.inc(v.tensor_mul(
                                        EbufM[:, (dg % 2) * 512:(dg % 2 + 1) * 512],
                                        Ebuf[:, (i % 3) * 512:(i % 3 + 1) * 512],
                                        masks[:, j * 512:(j + 1) * 512]))
                                ilast = tile_idx(hg, qc, 4 * qc + 3)
                                v.wait_ge(sPVT.h, ilast + 1)
                                for jq in range(4):
                                    g4 = hg * 4 * C + 4 * qc + jq
                                    if g4 >= 4:
                                        v.wait_ge(sYEV.h, g4 - 3)
                                    pv = ps_pv[(qc % 2) * 2 + jq // 2]
                                    sREC.inc(v.reciprocal(
                                        rec[:, jq:jq + 1],
                                        pv[:, (jq % 2) * 130 + 128:(jq % 2) * 130 + 129]))

                    @blk.sync
                    def _(s):
                        if ATTN < 5:
                            return
                        for hl in range(2):
                            hg = 2 * p + hl
                            s.wait_ge(sYTS.h, 4 * C * (hg + 1))
                            sYD.inc(s.dma_start(yin[hg].ap(), yT_send[:]), 16)

            if PHASES >= 2:
                emit_pass(0)
            if PHASES >= 3:
                emit_pass(1)

        with ExitStack() as wfctx:
            if PHASES < 4:
                return nc
            s_nat = wfctx.enter_context(nc.sbuf_tensor("s_nat", [128, TT_ * D], BF16))
            gtile = wfctx.enter_context(nc.sbuf_tensor("gtile", [128, 2 * 512], BF16))

            with ExitStack() as fctx:
                xnT = fctx.enter_context(nc.sbuf_tensor("xnT", [128, DT * Nq], BF16))

                with ExitStack() as wctx:
                    yT_own = wctx.enter_context(nc.sbuf_tensor("yT_own", [128, 16 * Nq], BF16))
                    wo_buf = wctx.enter_context(nc.sbuf_tensor("wo_buf", [128, 4 * 512], BF16))
                    xob = wctx.enter_context(nc.sbuf_tensor("xob", [128, DT * 512], BF16))
                    xsqF = wctx.enter_context(nc.sbuf_tensor("xsqF", [128, DT * 512], BF16))
                    rbF = wctx.enter_context(nc.sbuf_tensor("rbF", [128, 512], BF16))
                    mrbF = wctx.enter_context(nc.sbuf_tensor("mrbF", [128, 512], F32))
                    tmpF = wctx.enter_context(nc.sbuf_tensor("tmpF", [128, 512], F32))
                    mF = wctx.enter_context(nc.sbuf_tensor("mF", [1, 512], F32))
                    s2F = wctx.enter_context(nc.sbuf_tensor("s2F", [1, 512], F32))
                    m2F = wctx.enter_context(nc.sbuf_tensor("m2F", [1, 512], F32))
                    varF = wctx.enter_context(nc.sbuf_tensor("varF", [1, 512], F32))
                    lnvF = wctx.enter_context(nc.sbuf_tensor("lnvF", [1, 512], F32))
                    rstdF = wctx.enter_context(nc.sbuf_tensor("rstdF", [1, 512], F32))
                    mrsF = wctx.enter_context(nc.sbuf_tensor("mrsF", [1, 512], F32))
                    blkW = wctx.enter_context(nc.Block())
                    blkF1 = blkW
                    wp = ExitStack()
                    ps_w = [wp.enter_context(nc.psum_tensor(f"ps_w{i}", [128, 512], F32))
                            for i in range(TT_)]
                    @blkW.gpsimd
                    def _(g):
                        def emit_gather(i):
                            g.wait_ge(sAGY[i].h, 1)
                            for j in range(4):
                                sYO[i].inc(g.indirect_dma_start(
                                    yT_own[:, (i * 4 + j) * Nq:(i * 4 + j + 1) * Nq], None,
                                    yout[i].ap(),
                                    IndirectOffsetOnAxis(yix[:, j:j + 1], 1)), 16)

                        def emit_wo(k0, k1):
                            for kk in range(k0, k1):
                                oc, ft = kk // 16, kk % 16
                                if kk >= 4:
                                    g.wait_ge(sWOT.h, kk - 3)
                                sWO[kk % 4].inc(g.dma_start(
                                    wo_buf[:, (kk % 4) * 512:(kk % 4 + 1) * 512],
                                    wo[ft * 128:(ft + 1) * 128, oc * 512:(oc + 1) * 512]), 16)

                        for qi_ in range(4):
                            g.wait_ge(sQO[qi_].h, 16 * ((NG - qi_ + 3) // 4))
                        sAGW[2].inc(g.collective_compute(
                            "AllGather", ALU.bypass, replica_groups=RG_ALL,
                            ins=[gam_specs[2][4].ap().opt()],
                            outs=[gam_specs[2][5].ap().opt()]))
                        emit_gather(0)
                        emit_wo(0, 4)
                        emit_gather(1)
                        emit_wo(4, 8)
                        emit_gather(2)
                        emit_wo(8, 12)
                        emit_gather(3)
                        emit_wo(12, 64)

                    @blkW.tensor
                    def _(t):
                        for oc in range(4):
                            if oc >= 1:
                                t.wait_ge(sSEV.h, TT_ * oc)
                            for ft in range(16):
                                kk = oc * 16 + ft
                                if oc == 0 and ft % 4 == 0:
                                    t.wait_ge(sYO[ft // 4].h, 64)
                                t.wait_ge(sWO[kk % 4].h, 16 * (kk // 4 + 1))
                                for tt in range(TT_):
                                    ii = t.matmul(
                                        ps_w[tt][:],
                                        yT_own[:, ft * Nq + tt * 128:ft * Nq + (tt + 1) * 128],
                                        wo_buf[:, (kk % 4) * 512:(kk % 4 + 1) * 512],
                                        start=(ft == 0), stop=(ft == 15))
                                sWOT.inc(ii)

                    @blkW.scalar
                    def _(sc):
                        for oc in range(4):
                            sc.wait_ge(sWOT.h, 16 * (oc + 1))
                            for tt in range(TT_):
                                sSEV.inc(sc.activation(
                                    s_nat[:, tt * D + oc * 512:tt * D + (oc + 1) * 512],
                                    ps_w[tt][:], AF.Copy))

                    wp.close()
                    ps_fs = wctx.enter_context(nc.psum_tensor("ps_fs", [128, 512], F32))
                    ps_fq = wctx.enter_context(nc.psum_tensor("ps_fq", [128, 512], F32))
                    ps_fb = wctx.enter_context(nc.psum_tensor("ps_fb", [128, 512], F32))
                    if PHASES >= 5:
                        @blkF1.gpsimd
                        def _(g):
                            for tc in range(TC):
                                if tc >= 1:
                                    g.wait_ge(sXN.h, DT * tc)
                                sXO.inc(g.dma_start(
                                    xob[:],
                                    x_ownT[tc * 128:(tc + 1) * 128, :]),
                                    16)

                        @blkF1.tensor
                        def _(t):
                            t.wait_ge(sSEV.h, 4 * TT_)
                            for tc in range(TC):
                                t.wait_ge(sXO.h, 16 * (tc + 1))
                                if tc >= 1:
                                    t.wait_ge(sFR1.h, 2 * tc)
                                for dt in range(DT):
                                    t.matmul(ps_fs[0:1, 0:512], onesb[:, 0:1],
                                             xob[:, dt * 512:(dt + 1) * 512],
                                             start=(dt == 0), stop=(dt == DT - 1))
                                t.wait_ge(sFSQ.h, tc + 1)
                                for dt in range(DT):
                                    ii = t.matmul(ps_fq[0:1, 0:512], onesb[:, 0:1],
                                                  xsqF[:, dt * 512:(dt + 1) * 512],
                                                  start=(dt == 0), stop=(dt == DT - 1))
                                sFST.inc(ii)
                                t.wait_ge(sFRS.h, tc + 1)
                                if tc >= 1:
                                    t.wait_ge(sFBE.h, 2 * tc)
                                sFBC.inc(t.matmul(ps_fb[:, 0:512], onesf[0:1, :],
                                                  rstdF[0:1, :], start=True, stop=True))
                                t.wait_ge(sFR1.h, 2 * tc + 2)
                                t.wait_ge(sFBE.h, 2 * tc + 1)
                                sFBC.inc(t.matmul(ps_fb[:, 0:512], onesf[0:1, :],
                                                  mrsF[0:1, :], start=True, stop=True))

                        @blkF1.scalar
                        def _(sc):
                            for tc in range(TC):
                                sc.wait_ge(sXO.h, 16 * (tc + 1))
                                if tc >= 1:
                                    sc.wait_ge(sFST.h, tc)
                                for dt in range(DT):
                                    ii = sc.activation(xsqF[:, dt * 512:(dt + 1) * 512],
                                                       xob[:, dt * 512:(dt + 1) * 512],
                                                       AF.Square)
                                sFSQ.inc(ii)
                                sc.wait_ge(sFR1.h, 2 * tc + 1)
                                if tc >= 1:
                                    sc.wait_ge(sXN.h, DT * tc)
                                sSS.inc(sc.activation(varF[0:1, :], lnvF[0:1, :], AF.Ln))
                                sc.wait_ge(sSS.h, sSS.n)
                                sFRS.inc(sc.activation(rstdF[0:1, :], varF[0:1, :], AF.Exp,
                                                       scale=-0.5))
                                sc.wait_ge(sFBC.h, 2 * tc + 1)
                                sFBE.inc(sc.activation(rbF[:], ps_fb[:, 0:512], AF.Copy))
                                sc.wait_ge(sFBC.h, 2 * tc + 2)
                                sFBE.inc(sc.activation(mrbF[:], ps_fb[:, 0:512], AF.Copy))

                        @blkF1.vector
                        def _(v):
                            for tc in range(TC):
                                v.wait_ge(sFST.h, tc + 1)
                                if tc >= 1:
                                    v.wait_ge(sFR1.h, 2 * tc)
                                    v.wait_ge(sFRS.h, tc)
                                sVT.inc(v.tensor_scalar_mul(mF[0:1, :], ps_fs[0:1, 0:512],
                                                            1.0 / D))
                                sVT.inc(v.tensor_scalar_mul(s2F[0:1, :], ps_fq[0:1, 0:512],
                                                            1.0 / D))
                                v.wait_ge(sVT.h, sVT.n)
                                sVT.inc(v.tensor_mul(m2F[0:1, :], mF[0:1, :], mF[0:1, :]))
                                v.wait_ge(sVT.h, sVT.n)
                                sVT.inc(v.tensor_sub(varF[0:1, :], s2F[0:1, :], m2F[0:1, :]))
                                v.wait_ge(sVT.h, sVT.n)
                                sFR1.inc(v.tensor_scalar_add(lnvF[0:1, :], varF[0:1, :],
                                                             EPS_LN))
                                v.wait_ge(sFRS.h, tc + 1)
                                sFR1.inc(v.tensor_mul(mrsF[0:1, :], mF[0:1, :], rstdF[0:1, :]))
                                v.wait_ge(sFBE.h, 2 * tc + 2)
                                for dt in range(DT):
                                    if sXN.n > 0:
                                        v.wait_ge(sXN.h, sXN.n)
                                    sVT.inc(v.tensor_mul(tmpF[:],
                                                         xob[:, dt * 512:(dt + 1) * 512],
                                                         rbF[:]))
                                    v.wait_ge(sVT.h, sVT.n)
                                    sXN.inc(v.tensor_sub(
                                        xnT[:, dt * Nq + tc * 512:dt * Nq + (tc + 1) * 512],
                                        tmpF[:], mrbF[:]))

                if PHASES < 5:
                    return nc
                f23ctx = fctx.enter_context(ExitStack())
                hidT = f23ctx.enter_context(nc.sbuf_tensor("hidT", [128, HT * Nq], BF16))
                with ExitStack() as f2ctx:
                    w1tb = f2ctx.enter_context(nc.sbuf_tensor("w1tb", [128, 2 * 2048], BF16))
                    w2tb = f2ctx.enter_context(nc.sbuf_tensor("w2tb", [128, 2 * 2048], BF16))
                    sgb = f2ctx.enter_context(nc.sbuf_tensor("sgb", [128, 2 * Nq], BF16))
                    x2b = f2ctx.enter_context(nc.sbuf_tensor("x2b", [128, 2 * Nq], BF16))
                    ps_f = [f2ctx.enter_context(nc.psum_tensor(f"ps_f{i}", [128, 512], F32))
                            for i in range(2 * 2 * TC)]
                    blkF2 = f2ctx.enter_context(nc.Block())

                    @blkF2.sync
                    def _(s):
                        s.wait_ge(sAGW[0].h, 1)
                        s.wait_ge(sAGW[1].h, 1)
                        for ht in range(HT):
                            if ht >= 2:
                                s.wait_ge(sFM.h, 2 * TC * (ht - 1))
                            sW12[ht % 2].inc(s.dma_start(
                                w1tb[:, (ht % 2) * 2048:(ht % 2 + 1) * 2048],
                                w1qo.rearrange("(c d h) (p x) -> p c d h x",
                                               c=8, d=2, p=128)[:, :, :, ht, :]), 16)
                            sW12[ht % 2].inc(s.dma_start(
                                w2tb[:, (ht % 2) * 2048:(ht % 2 + 1) * 2048],
                                w2qo.rearrange("(c d h) (p x) -> p c d h x",
                                               c=8, d=2, p=128)[:, :, :, ht, :]), 16)

                    @blkF2.tensor
                    def _(t):
                        t.wait_ge(sXN.h, DT * TC)
                        for ht in range(HT):
                            t.wait_ge(sW12[ht % 2].h, 32 * (ht // 2 + 1))
                            if ht >= 2:
                                t.wait_ge(sFE.h, 2 * TC * (ht - 1))
                            for mi, wtb in enumerate((w1tb, w2tb)):
                                for dt in range(DT):
                                    lhs = wtb[:, (ht % 2) * 2048 + dt * 128:(ht % 2) * 2048 + (dt + 1) * 128]
                                    for tc in range(TC):
                                        ii = t.matmul(
                                            ps_f[(ht % 2) * 2 * TC + mi * TC + tc][:],
                                            lhs,
                                            xnT[:, dt * Nq + tc * 512:dt * Nq + (tc + 1) * 512],
                                            start=(dt == 0), stop=(dt == DT - 1))
                                        if dt == DT - 1:
                                            sFM.inc(ii)

                    @blkF2.scalar
                    def _(sc):
                        for ht in range(HT):
                            if ht >= 2:
                                sc.wait_ge(sHID.h, TC * (ht - 1))
                            for tc in range(TC):
                                sc.wait_ge(sFM.h, 2 * TC * ht + tc + 1)
                                sFE.inc(sc.activation(
                                    sgb[:, (ht % 2) * Nq + tc * 512:(ht % 2) * Nq + (tc + 1) * 512],
                                    ps_f[(ht % 2) * 2 * TC + 0 * TC + tc][:], AF.Sigmoid))
                            for tc in range(TC):
                                sc.wait_ge(sFM.h, 2 * TC * ht + TC + tc + 1)
                                sFE.inc(sc.activation(
                                    x2b[:, (ht % 2) * Nq + tc * 512:(ht % 2) * Nq + (tc + 1) * 512],
                                    ps_f[(ht % 2) * 2 * TC + 1 * TC + tc][:], AF.Copy))

                    @blkF2.vector
                    def _(v):
                        for ht in range(HT):
                            v.wait_ge(sFE.h, 2 * TC * (ht + 1))
                            for tc in range(TC):
                                sHID.inc(v.tensor_mul(
                                    hidT[:, ht * Nq + tc * 512:ht * Nq + (tc + 1) * 512],
                                    sgb[:, (ht % 2) * Nq + tc * 512:(ht % 2) * Nq + (tc + 1) * 512],
                                    x2b[:, (ht % 2) * Nq + tc * 512:(ht % 2) * Nq + (tc + 1) * 512]))

                with ExitStack() as f3ctx:
                    w3rb = f3ctx.enter_context(nc.sbuf_tensor("w3rb", [128, 4 * 512], BF16))
                    pbuf = f3ctx.enter_context(nc.sbuf_tensor("pbuf", [128, 512], BF16))
                    ps_g3 = [f3ctx.enter_context(nc.psum_tensor(f"ps_g3{i}", [128, 512], F32))
                             for i in range(TT_)]
                    blkF3 = f3ctx.enter_context(nc.Block())

                    @blkF3.sync
                    def _(s):
                        s.wait_ge(sAGW[2].h, 1)
                        for og in range(4):
                            for ht in range(HT):
                                kk = og * HT + ht
                                if kk >= 4:
                                    s.wait_ge(sW3T.h, kk - 3)
                                sW3L[kk % 4].inc(s.dma_start(
                                    w3rb[:, (kk % 4) * 512:(kk % 4 + 1) * 512],
                                    w3qo.rearrange("(g h) (p x) -> p g h x",
                                                   g=8, p=128)[:, 2 * og:2 * og + 2, ht, :]),
                                    16)

                    @blkF3.tensor
                    def _(t):
                        t.wait_ge(sHID.h, TC * HT)
                        for og in range(4):
                            if og >= 1:
                                t.wait_ge(sGE.h, TT_ * og)
                            for ht in range(HT):
                                kk = og * HT + ht
                                t.wait_ge(sW3L[kk % 4].h, 16 * (kk // 4 + 1))
                                for t8 in range(TT_):
                                    ii = t.matmul(
                                        ps_g3[t8][:],
                                        hidT[:, ht * Nq + t8 * 128:ht * Nq + (t8 + 1) * 128],
                                        w3rb[:, (kk % 4) * 512:(kk % 4 + 1) * 512],
                                        start=(ht == 0), stop=(ht == HT - 1))
                                sW3T.inc(ii)

                    @blkF3.scalar
                    def _(sc):
                        for og in range(4):
                            sc.wait_ge(sW3T.h, HT * (og + 1))
                            for t8 in range(TT_):
                                q = og * TT_ + t8
                                if q >= 2:
                                    sc.wait_ge(sSG.h, q - 1)
                                sGE.inc(sc.activation(
                                    gtile[:, (q % 2) * 512:(q % 2 + 1) * 512],
                                    ps_g3[t8][:], AF.Silu))

                    @blkF3.vector
                    def _(v):
                        for og in range(4):
                            for t8 in range(TT_):
                                q = og * TT_ + t8
                                v.wait_ge(sGE.h, q + 1)
                                if q >= 1:
                                    v.wait_ge(sSG.h, q)
                                sVT.inc(v.tensor_mul(
                                    pbuf[:],
                                    gtile[:, (q % 2) * 512:(q % 2 + 1) * 512],
                                    s_nat[:, t8 * D + og * 512:t8 * D + (og + 1) * 512]))
                                v.wait_ge(sVT.h, sVT.n)
                                sSG.inc(v.tensor_copy(
                                    s_nat[:, t8 * D + og * 512:t8 * D + (og + 1) * 512],
                                    pbuf[:]))

            if PHASES < 6:
                return nc
            with ExitStack() as zctx:
                xbuf = zctx.enter_context(nc.sbuf_tensor("xbuf", [128, 2 * D], F32))
                sgf = zctx.enter_context(nc.sbuf_tensor("sgf", [128, D], F32))
                obuf = zctx.enter_context(nc.sbuf_tensor("obuf", [128, 2 * D], F32))
                blkZ = zctx.enter_context(nc.Block())

                @blkZ.sync
                def _(s):
                    for tt in range(TT_ + 2):
                        if tt < TT_:
                            if tt >= 2:
                                s.wait_ge(sVF.h, tt - 1)
                            sXL[tt % 2].inc(s.dma_start(
                                xbuf[:, (tt % 2) * D:(tt % 2 + 1) * D],
                                x_own[tt * 128:(tt + 1) * 128, :]), 16)
                        if tt >= 2:
                            ts_ = tt - 2
                            s.wait_ge(sVF.h, ts_ + 1)
                            sOUT[ts_ % 2].inc(s.dma_start(
                                out[ts_ * 128:(ts_ + 1) * 128, :],
                                obuf[:, (ts_ % 2) * D:(ts_ % 2 + 1) * D]), 16)
                    s.wait_ge(sOUT[0].h, 16 * ((TT_ + 1) // 2))
                    s.wait_ge(sOUT[1].h, 16 * (TT_ // 2))

                @blkZ.vector
                def _(v):
                    v.wait_ge(sSG.h, 4 * TT_)
                    for tt in range(TT_):
                        v.wait_ge(sXL[tt % 2].h, 16 * (tt // 2 + 1))
                        if tt >= 2:
                            v.wait_ge(sOUT[tt % 2].h, 16 * ((tt - 2) // 2 + 1))
                        if tt >= 1:
                            v.wait_ge(sVF.h, tt)
                        sVT.inc(v.tensor_copy(sgf[:], s_nat[:, tt * D:(tt + 1) * D]))
                        v.wait_ge(sVT.h, sVT.n)
                        sVF.inc(v.tensor_add(obuf[:, (tt % 2) * D:(tt % 2 + 1) * D],
                                             xbuf[:, (tt % 2) * D:(tt % 2 + 1) * D],
                                             sgf[:]))

    return nc


def _rope_perm():
    p = np.zeros(128, dtype=np.int64)
    p[0:64] = np.arange(0, 128, 2)
    p[64:128] = np.arange(1, 128, 2)
    return p


def prep_inputs(x, W_qkv, W_o, w1, w2, w3, N, H):
    Nq = N // 4
    HT = H // 128
    perm = _rope_perm()

    dh = 128
    theta = 1.0 / (10000.0 ** (np.arange(0, dh, 2, dtype=np.float64) / dh))
    fr = np.arange(N, dtype=np.float64)[:, None] * theta[None, :]
    scale = dh ** -0.25
    cosin = np.empty((128, N), dtype=np.float32)
    cosin[0:64, :] = (np.cos(fr).T * scale).astype(np.float32)
    cosin[64:128, :] = (np.sin(fr).T * scale).astype(np.float32)
    cosin2 = np.empty((128, N), dtype=np.float32)
    cosin2[0:64, :] = (-np.sin(fr).T * scale).astype(np.float32)
    cosin2[64:128, :] = (np.cos(fr).T * scale).astype(np.float32)

    mask = np.zeros((128, 4 * 512), dtype=np.float32)
    for j in range(4):
        mm, qq = np.meshgrid(np.arange(128), np.arange(512), indexing="ij")
        mask[:, j * 512:(j + 1) * 512] = (qq >= mm + 128 * j).astype(np.float32)
    masksd = mask.astype(BFNP)

    ones2 = np.ones((128, 128), dtype=np.float32)
    iden = np.eye(128, dtype=np.float32)

    wo_rows = []
    for ft in range(16):
        head = 4 * (ft % 4) + ft // 4
        wo_rows.append(W_o[head * 128:(head + 1) * 128, :])
    wo_p = np.ascontiguousarray(np.concatenate(wo_rows, axis=0))

    w1T = np.ascontiguousarray(w1.T)
    w2T = np.ascontiguousarray(w2.T)
    w3T = np.ascontiguousarray(w3.T)

    in_maps = []
    for c in range(8):
        b, qi = c // 4, c % 4
        xb = x[b]
        C_ = N // 512
        xTc = np.ascontiguousarray(
            xb.reshape(C_, 512, 16, 128).transpose(0, 3, 2, 1).reshape(C_ * 128, 16 * 512))
        x_own = np.ascontiguousarray(xb[qi * Nq:(qi + 1) * Nq, :])
        TCq = max(1, Nq // 512)
        x_ownT = np.ascontiguousarray(
            x_own.reshape(TCq, 512, 16, 128).transpose(0, 3, 2, 1).reshape(TCq * 128, 16 * 512))

        cols = []
        for pp in range(2):
            for kind in range(3):
                for hl in range(2):
                    h = 4 * qi + 2 * pp + hl
                    base = kind * (16 * 128)
                    cc = W_qkv[:, base + h * 128: base + (h + 1) * 128]
                    if kind < 2:
                        cc = cc[:, perm]
                    cols.append(cc)
        wqkv_c = np.ascontiguousarray(np.concatenate(cols, axis=1))

        w1s = np.ascontiguousarray(
            w1T[256 * c:256 * (c + 1), :].reshape(2, 128, HT, 128)
            .transpose(0, 2, 1, 3).reshape(2 * HT, 128 * 128))
        w2s = np.ascontiguousarray(
            w2T[256 * c:256 * (c + 1), :].reshape(2, 128, HT, 128)
            .transpose(0, 2, 1, 3).reshape(2 * HT, 128 * 128))
        w3s = np.ascontiguousarray(
            w3T[:, 256 * c:256 * (c + 1)].reshape(HT, 128, 256)
            .reshape(HT, 128 * 256))

        P = np.arange(128, dtype=np.int64)[:, None]
        yix = ((np.arange(4)[None, :] * 128 + P) * N + qi * Nq).astype(np.int32)

        in_maps.append(dict(
            xT=xTc, x_own=x_own, x_ownT=x_ownT, wqkv=wqkv_c, wo=wo_p,
            w1s=w1s, w2s=w2s, w3s=w3s, cosin=cosin, masksd=masksd,
            ones2=ones2, iden=iden, yidx=yix, cosin2=cosin2,
        ))
    return in_maps


def kernel(L0_attn_x, L0_attn_W_qkv, L0_attn_W_o, L0_ffn_w1, L0_ffn_w2, L0_ffn_w3):
    x = np.asarray(L0_attn_x, dtype=np.float32)
    N = x.shape[1]
    H = np.asarray(L0_ffn_w1).shape[0]
    Nq = N // 4
    nc = build_nc(N, H)
    in_maps = prep_inputs(
        x, np.asarray(L0_attn_W_qkv, np.float32), np.asarray(L0_attn_W_o, np.float32),
        np.asarray(L0_ffn_w1, np.float32), np.asarray(L0_ffn_w2, np.float32),
        np.asarray(L0_ffn_w3, np.float32), N, H)
    res = run_bass_kernel_spmd(nc, in_maps, core_ids=list(range(8)))
    outp = np.empty((2, N, D), dtype=np.float32)
    for c in range(8):
        b, qi = c // 4, c % 4
        outp[b, qi * Nq:(qi + 1) * Nq, :] = np.asarray(res.results[c]["out"],
                                                       dtype=np.float32)
    return outp


# version 11
# speedup vs baseline: 1.0239x; 1.0239x over previous
import numpy as np
import ml_dtypes
from contextlib import ExitStack

import concourse.bass as bass
import concourse.mybir as mybir
from concourse.bass_utils import run_bass_kernel_spmd
from concourse.bass_core_rust import IndirectOffsetOnAxis

F32 = mybir.dt.float32
BF16 = mybir.dt.bfloat16
I32 = mybir.dt.int32
AF = mybir.ActivationFunctionType
ALU = mybir.AluOpType
AX = mybir.AxisListType
BFNP = ml_dtypes.bfloat16

D = 2048
DT = D // 128
EPS_LN = 1e-5
EPS_Q = 1e-8


class Sem:
    def __init__(self, ctx, nc, name):
        self.h = ctx.enter_context(nc.semaphore(name))
        self.n = 0

    def inc(self, inst, k=1):
        inst.then_inc(self.h, k)
        self.n += k
        return inst


def build_nc(N, H):
    import os
    PHASES = int(os.environ.get("MK_PHASES", "9"))
    ATTN = int(os.environ.get("MK_ATTN", "9"))
    C = N // 512
    KT = N // 128
    Nq = N // 4
    TT_ = Nq // 128
    TC = max(1, Nq // 512)
    HT = H // 128
    Mh = 2 * C * (C + 1)
    QGRP = float(H * D)
    NG1 = (2 * HT * 128 + 4095) // 4096
    NG3 = (HT * 256 + 4095) // 4096
    NG = 2 * NG1 + NG3

    nc = bass.Bass("TRN2", target_bir_lowering=False)

    xT = nc.dram_tensor("xT", [(N // 512) * 128, DT * 512], F32, kind="ExternalInput").ap()
    x_own = nc.dram_tensor("x_own", [Nq, D], F32, kind="ExternalInput").ap()
    x_ownT = nc.dram_tensor("x_ownT", [max(1, Nq // 512) * 128, DT * 512], F32, kind="ExternalInput").ap()
    wqkv = nc.dram_tensor("wqkv", [D, 2 * 768], F32, kind="ExternalInput").ap()
    wo = nc.dram_tensor("wo", [D, D], F32, kind="ExternalInput").ap()
    w1s = nc.dram_tensor("w1s", [128, 2 * HT * 128], F32, kind="ExternalInput").ap()
    w2s = nc.dram_tensor("w2s", [128, 2 * HT * 128], F32, kind="ExternalInput").ap()
    w3s = nc.dram_tensor("w3s", [128, HT * 256], F32, kind="ExternalInput").ap()
    cosin = nc.dram_tensor("cosin", [128, N], F32, kind="ExternalInput").ap()
    cosin2 = nc.dram_tensor("cosin2", [128, N], F32, kind="ExternalInput").ap()
    masksd = nc.dram_tensor("masksd", [128, 4 * 512], BF16, kind="ExternalInput").ap()
    ones2 = nc.dram_tensor("ones2", [128, 128], F32, kind="ExternalInput").ap()
    iden = nc.dram_tensor("iden", [128, 128], F32, kind="ExternalInput").ap()
    yidx = nc.dram_tensor("yidx", [128, 4], I32, kind="ExternalInput").ap()

    out = nc.dram_tensor("out", [Nq, D], F32, kind="ExternalOutput").ap()

    ar_in = nc.dram_tensor("ar_in", [1, 8], F32)
    ar_out = nc.dram_tensor("ar_out", [1, 8], F32)
    w1qi = nc.dram_tensor("w1qi", [128, 2 * HT * 128], BF16)
    w1qo = nc.dram_tensor("w1qo", [8 * 128, 2 * HT * 128], BF16)
    w2qi = nc.dram_tensor("w2qi", [128, 2 * HT * 128], BF16)
    w2qo = nc.dram_tensor("w2qo", [8 * 128, 2 * HT * 128], BF16)
    w3qi = nc.dram_tensor("w3qi", [128, HT * 256], BF16)
    w3qo = nc.dram_tensor("w3qo", [8 * 128, HT * 256], BF16)
    yin = [nc.dram_tensor(f"yin{i}", [128, N], BF16) for i in range(4)]
    yout = [nc.dram_tensor(f"yout{i}", [512, N], BF16) for i in range(4)]

    RG_ALL = [list(range(8))]
    RG_HALF = [[0, 1, 2, 3], [4, 5, 6, 7]]

    ctx = ExitStack()
    with ctx:
        S_ = lambda name: Sem(ctx, nc, name)
        sCONST = S_("sCONST"); sWQ = S_("sWQ"); sCM = S_("sCM"); sVN1 = S_("sVN1")
        sGW = S_("sGW"); sGWg = [S_(f"sGWg{i}") for i in range(4)]
        sQO = [S_(f"sQO{i}") for i in range(4)]
        sGR = S_("sGR"); sGMM = S_("sGMM"); sGEV = S_("sGEV")
        sGDMA = S_("sGDMA"); sARC = S_("sARC"); sSCV = S_("sSCV"); sQV = S_("sQV")
        sAGW = [S_(f"sAGW{i}") for i in range(3)]
        sXT = [S_(f"sXT{i}") for i in range(2)]
        sSQ = S_("sSQ"); sSQ2 = S_("sSQ2"); sSTAT = S_("sSTAT")
        sROW1 = S_("sROW1"); sROWS = S_("sROWS"); sROW2 = S_("sROW2")
        sBC = S_("sBC"); sBCE = S_("sBCE"); sRT = S_("sRT"); sRTE = S_("sRTE")
        sWS = S_("sWS"); sWST = S_("sWST")
        sQK = S_("sQK"); sPRE = S_("sPRE"); sROPE = S_("sROPE"); sCSR = S_("sCSR")
        sVM = S_("sVM"); sVEV = S_("sVEV")
        sST = S_("sST"); sEXP = S_("sEXP"); sMSK = S_("sMSK"); sPVT = S_("sPVT")
        sREC = S_("sREC"); sYEV = S_("sYEV"); sYTR = S_("sYTR"); sYTS = S_("sYTS")
        sYD = S_("sYD")
        sAGY = [S_(f"sAGY{i}") for i in range(4)]
        sYO = [S_(f"sYO{i}") for i in range(4)]
        sWO = [S_(f"sWO{i}") for i in range(4)]
        sWOT = S_("sWOT"); sSEV = S_("sSEV")
        sXO = S_("sXO"); sFSQ = S_("sFSQ"); sFST = S_("sFST"); sFR1 = S_("sFR1")
        sFRS = S_("sFRS"); sFBC = S_("sFBC"); sFBE = S_("sFBE"); sXN = S_("sXN")
        sW12 = [S_(f"sW12_{i}") for i in range(2)]
        sFM = S_("sFM"); sFE = S_("sFE"); sHID = S_("sHID")
        sW3L = [S_(f"sW3L{i}") for i in range(4)]
        sW3T = S_("sW3T"); sGE = S_("sGE"); sSG = S_("sSG")
        sXL = [S_(f"sXL{i}") for i in range(2)]
        sVF = S_("sVF"); sOUT = [S_(f"sOUT{i}") for i in range(2)]
        sVT = S_("sVT"); sSS = S_("sSS")

        idf = ctx.enter_context(nc.sbuf_tensor("idf", [128, 128], F32))
        idb = ctx.enter_context(nc.sbuf_tensor("idb", [128, 128], BF16))
        onesf = ctx.enter_context(nc.sbuf_tensor("onesf", [128, 128], F32))
        onesb = ctx.enter_context(nc.sbuf_tensor("onesb", [128, 128], BF16))
        rstd_col = ctx.enter_context(nc.sbuf_tensor("rstd_col", [128, KT], F32))
        scvec = ctx.enter_context(nc.sbuf_tensor("scvec", [128, 4], F32))
        garow = ctx.enter_context(nc.sbuf_tensor("garow", [1, 8], F32))
        gsrow = ctx.enter_context(nc.sbuf_tensor("gsrow", [1, 8], F32))
        gtmp2 = ctx.enter_context(nc.sbuf_tensor("gtmp2", [1, 4], F32))
        gtmp3 = ctx.enter_context(nc.sbuf_tensor("gtmp3", [1, 4], F32))
        gscal = ctx.enter_context(nc.sbuf_tensor("gscal", [1, 4], F32))
        gacc = ctx.enter_context(nc.sbuf_tensor("gacc", [128, 8], F32))
        gaccF = ctx.enter_context(nc.sbuf_tensor("gaccF", [128, 4], F32))
        gtmp = ctx.enter_context(nc.sbuf_tensor("gtmp", [128, 1], F32))
        yix = ctx.enter_context(nc.sbuf_tensor("yix", [128, 4], I32))

        gam_specs = [
            (w1s, 2 * HT * 128, 1, 4096, w1qi, w1qo),
            (w2s, 2 * HT * 128, 1, 4096, w2qi, w2qo),
            (w3s, HT * 256, 1, 4096, w3qi, w3qo),
        ]
        groups = []
        for si, (din, rows, relC, grows, dq, dout) in enumerate(gam_specs):
            for r0 in range(0, rows, grows):
                groups.append((si, r0, min(grows, rows - r0)))
        assert len(groups) == NG
        gcnt = [sum(1 for g in groups if g[0] == si) for si in range(3)]

        with ExitStack() as pctx:
            gbuf = pctx.enter_context(nc.sbuf_tensor("gbuf", [128, 4 * 4096], F32))
            qbuf = pctx.enter_context(nc.sbuf_tensor("qbuf", [128, 4 * 4096], BF16))
            qa = pctx.enter_context(nc.sbuf_tensor("qa", [128, 4096], F32))
            qb = pctx.enter_context(nc.sbuf_tensor("qb", [128, 4096], F32))
            ps_g = pctx.enter_context(nc.psum_tensor("ps_g", [128, 512], F32))
            blkA = pctx.enter_context(nc.Block())

            @blkA.gpsimd
            def _(g):
                sCONST.inc(g.dma_start(idb[:], iden), 16)
                sCONST.inc(g.dma_start(onesb[:], ones2), 16)
                sCONST.inc(g.dma_start(yix[:], yidx), 16)
                g.wait_ge(sGDMA.h, 16)
                sARC.inc(g.collective_compute(
                    "AllReduce", ALU.add, replica_groups=RG_ALL,
                    ins=[ar_in.ap().opt()], outs=[ar_out.ap().opt()]))
                for qi_ in range(4):
                    g.wait_ge(sQO[qi_].h, 16 * ((NG - qi_ + 3) // 4))

            @blkA.sync
            def _(s):
                sGW.inc(s.dma_start(idf[:], iden), 16)
                sGW.inc(s.dma_start(onesf[:], ones2), 16)
                for gi, (si, r0, nr) in enumerate(groups):
                    din, rows, relC, grows, dq, dout = gam_specs[si]
                    if gi >= 4:
                        s.wait_ge(sGR.h, gi - 3)
                    sGWg[gi % 4].inc(s.dma_start(
                        gbuf[:, (gi % 4) * 4096:(gi % 4) * 4096 + nr * relC],
                        din[:, r0:r0 + nr]), 16)
                s.wait_ge(sGEV.h, 1)
                sGDMA.inc(s.dma_start(ar_in.ap(), garow[:]), 16)
                s.wait_ge(sARC.h, 1)
                sGDMA.inc(s.dma_start(gsrow[:], ar_out.ap()), 16)
                for gi, (si, r0, nr) in enumerate(groups):
                    din, rows, relC, grows, dq, dout = gam_specs[si]
                    if gi < 4:
                        s.wait_ge(sGR.h, NG)
                    else:
                        s.wait_ge(sQV.h, gi - 3)
                    sGWg[(NG + gi) % 4].inc(s.dma_start(
                        gbuf[:, (gi % 4) * 4096:(gi % 4) * 4096 + nr * relC],
                        din[:, r0:r0 + nr]), 16)
                    s.wait_ge(sQV.h, gi + 1)
                    sQO[gi % 4].inc(s.dma_start(
                        dq[:, r0:r0 + nr],
                        qbuf[:, (gi % 4) * 4096:(gi % 4) * 4096 + nr * relC]), 16)

            @blkA.tensor
            def _(t):
                t.wait_ge(sGW.h, 32)
                t.wait_ge(sVT.h, NG + 3)
                sGMM.inc(t.matmul(ps_g[0:1, 0:3], onesf[:, 0:1], gaccF[:, 0:3],
                                  start=True, stop=True))
                t.wait_ge(sSCV.h, 1)
                t.wait_ge(sGEV.h, 1)
                sGMM.inc(t.matmul(ps_g[:, 0:3], onesf[0:1, :], gscal[0:1, 0:3],
                                  start=True, stop=True))

            @blkA.scalar
            def _(sc):
                sSS.inc(sc.memzero(garow[0:1, 0:8]))
                sc.wait_ge(sGMM.h, 1)
                sc.wait_ge(sSS.h, sSS.n)
                sGEV.inc(sc.activation(garow[0:1, 0:3], ps_g[0:1, 0:3], AF.Copy))
                sc.wait_ge(sGMM.h, 2)
                sGEV.inc(sc.activation(scvec[:, 0:3], ps_g[:, 0:3], AF.Copy))

            @blkA.vector
            def _(v):
                cnt = [0, 0, 0]
                for gi, (si, r0, nr) in enumerate(groups):
                    relC = gam_specs[si][2]
                    v.wait_ge(sGWg[gi % 4].h, 16 * (gi // 4 + 1))
                    if gi >= 1:
                        v.wait_ge(sVT.h, sVT.n)
                    i1 = v.tensor_reduce(
                        gtmp[:], gbuf[:, (gi % 4) * 4096:(gi % 4) * 4096 + nr * relC],
                        AX.X, ALU.add, apply_absolute_value=True)
                    sGR.inc(i1)
                    v.wait_ge(sGR.h, sGR.n)
                    if cnt[si] == 0:
                        sVT.inc(v.tensor_copy(gacc[:, si * 2:si * 2 + 1], gtmp[:]))
                    else:
                        psrc = si * 2 + (cnt[si] - 1) % 2
                        pdst = si * 2 + cnt[si] % 2
                        sVT.inc(v.tensor_add(gacc[:, pdst:pdst + 1],
                                             gacc[:, psrc:psrc + 1], gtmp[:]))
                    cnt[si] += 1
                for si in range(3):
                    pfin = si * 2 + (cnt[si] - 1) % 2
                    v.wait_ge(sVT.h, sVT.n)
                    sVT.inc(v.tensor_copy(gaccF[:, si:si + 1],
                                          gacc[:, pfin:pfin + 1]))
                v.wait_ge(sGDMA.h, 32)
                sVT.inc(v.tensor_scalar_mul(gtmp2[0:1, 0:3], gsrow[0:1, 0:3],
                                            1.0 / QGRP))
                v.wait_ge(sVT.h, sVT.n)
                sVT.inc(v.tensor_scalar_add(gtmp3[0:1, 0:3], gtmp2[0:1, 0:3], EPS_Q))
                v.wait_ge(sVT.h, sVT.n)
                sSCV.inc(v.reciprocal(gscal[0:1, 0:3], gtmp3[0:1, 0:3]))
                v.wait_ge(sGEV.h, 2)
                for gi, (si, r0, nr) in enumerate(groups):
                    relC = gam_specs[si][2]
                    w = nr * relC
                    v.wait_ge(sGWg[(NG + gi) % 4].h, 16 * ((NG + gi) // 4 + 1))
                    if gi >= 4:
                        v.wait_ge(sQO[gi % 4].h, 16 * (gi // 4))
                    if gi >= 1:
                        v.wait_ge(sQV.h, gi)
                    src = gbuf[:, (gi % 4) * 4096:(gi % 4) * 4096 + w]
                    i1 = v.tensor_scalar(qa[:, 0:w], src, scvec[:, si:si + 1], 0.5,
                                         ALU.mult, ALU.is_ge)
                    sVT.inc(i1)
                    i2 = v.tensor_scalar(qb[:, 0:w], src, scvec[:, si:si + 1], -0.5,
                                         ALU.mult, ALU.is_le)
                    sVT.inc(i2)
                    v.wait_ge(sVT.h, sVT.n)
                    i3 = v.tensor_sub(qbuf[:, (gi % 4) * 4096:(gi % 4) * 4096 + w],
                                      qa[:, 0:w], qb[:, 0:w])
                    sQV.inc(i3)

        with ExitStack() as actx:
            wqkv_sb = actx.enter_context(nc.sbuf_tensor("wqkv_sb", [128, DT * 768], BF16))
            cs_sb = actx.enter_context(nc.sbuf_tensor("cs_sb", [128, N], BF16))
            cs2_sb = actx.enter_context(nc.sbuf_tensor("cs2_sb", [128, N], BF16))
            QTs = actx.enter_context(nc.sbuf_tensor("QTs", [128, 2 * N], BF16))
            KTs = actx.enter_context(nc.sbuf_tensor("KTs", [128, 2 * N], BF16))
            Vn = actx.enter_context(nc.sbuf_tensor("Vn", [128, 2 * KT * 132], BF16))
            xTb = actx.enter_context(nc.sbuf_tensor("xTb", [128, 2 * DT * 512], BF16))
            xsqb = actx.enter_context(nc.sbuf_tensor("xsqb", [128, (DT // 2) * 512], BF16))
            masks = actx.enter_context(nc.sbuf_tensor("masks", [128, 4 * 512], BF16))
            prer_e = actx.enter_context(nc.sbuf_tensor("prer_e", [128, 2 * 512], BF16))
            prer_o = actx.enter_context(nc.sbuf_tensor("prer_o", [128, 2 * 512], BF16))
            rt1 = actx.enter_context(nc.sbuf_tensor("rt1", [128, 512], BF16))
            rt2 = actx.enter_context(nc.sbuf_tensor("rt2", [128, 512], BF16))
            rbc = actx.enter_context(nc.sbuf_tensor("rbc", [128, 2 * 512], BF16))
            cosr = actx.enter_context(nc.sbuf_tensor("cosr", [128, 512], BF16))
            sinr = actx.enter_context(nc.sbuf_tensor("sinr", [128, 512], BF16))
            Ebuf = actx.enter_context(nc.sbuf_tensor("Ebuf", [128, 3 * 512], BF16))
            EbufM = actx.enter_context(nc.sbuf_tensor("EbufM", [128, 2 * 512], BF16))
            ynat = actx.enter_context(nc.sbuf_tensor("ynat", [128, 4 * 128], BF16))
            rec = actx.enter_context(nc.sbuf_tensor("rec", [128, 4], F32))
            yT_send = actx.enter_context(nc.sbuf_tensor("yT_send", [128, N], BF16))
            m_row = actx.enter_context(nc.sbuf_tensor("m_row", [1, N], BF16))
            rstd_row = actx.enter_context(nc.sbuf_tensor("rstd_row", [1, N], BF16))
            ws_neg = actx.enter_context(nc.sbuf_tensor("ws_neg", [1, 768], BF16))
            m_f32 = actx.enter_context(nc.sbuf_tensor("m_f32", [1, 512], F32))
            s2f = actx.enter_context(nc.sbuf_tensor("s2f", [1, 512], F32))
            m2f = actx.enter_context(nc.sbuf_tensor("m2f", [1, 512], F32))
            varf = actx.enter_context(nc.sbuf_tensor("varf", [1, 512], F32))
            lnvf = actx.enter_context(nc.sbuf_tensor("lnvf", [1, 512], F32))
            rstdf = actx.enter_context(nc.sbuf_tensor("rstdf", [1, 512], F32))

            def emit_pass(p):
                with ExitStack() as qctx:
                    ps_stats = qctx.enter_context(nc.psum_tensor(f"ps_stats{p}", [128, 512], F32))
                    ps_bc = qctx.enter_context(nc.psum_tensor(f"ps_bc{p}", [128, 512], F32))
                    ps_rt = qctx.enter_context(nc.psum_tensor(f"ps_rt{p}", [128, 128], BF16))
                    ps_yt1 = qctx.enter_context(nc.psum_tensor(f"ps_yt1{p}", [128, 128], BF16))
                    ps_qk = [qctx.enter_context(nc.psum_tensor(f"ps_qk{p}_{i}", [128, 512], F32)) for i in range(2)]
                    ps_v = [qctx.enter_context(nc.psum_tensor(f"ps_v{p}_{i}", [128, 512], F32)) for i in range(2)]
                    blk = qctx.enter_context(nc.Block())

                    @blk.gpsimd
                    def _(g):
                        if p == 0:
                            sWQ.inc(g.dma_start(
                                wqkv_sb[:],
                                wqkv.rearrange("(dt p) f -> p dt f", p=128)[:, :, 0:768]),
                                16)
                            sCM.inc(g.dma_start(cs_sb[:], cosin), 16)
                            sCM.inc(g.dma_start(cs2_sb[:], cosin2), 16)
                            sCM.inc(g.dma_start(masks[:], masksd), 16)
                        if p == 1:
                            sWQ.inc(g.dma_start(
                                wqkv_sb[:],
                                wqkv.rearrange("(dt p) f -> p dt f", p=128)[:, :, 768:1536]),
                                16)
                            for qi_ in range(4):
                                g.wait_ge(sQO[qi_].h, 16 * ((NG - qi_ + 3) // 4))
                            for si in range(2):
                                sAGW[si].inc(g.collective_compute(
                                    "AllGather", ALU.bypass, replica_groups=RG_ALL,
                                    ins=[gam_specs[si][4].ap().opt()],
                                    outs=[gam_specs[si][5].ap().opt()]))
                        for c in range(C):
                            k = C * p + c
                            if k >= 2:
                                g.wait_ge(sVM.h, 4 * (k - 1))
                                if p == 0 and c >= 2:
                                    g.wait_ge(sSQ.h, 2 * (c - 1))
                            sXT[k % 2].inc(g.dma_start(
                                xTb[:, (c % 2) * DT * 512:(c % 2 + 1) * DT * 512],
                                xT[c * 128:(c + 1) * 128, :]),
                                16)
                        if ATTN < 5:
                            return
                        for hl in range(2):
                            hg = 2 * p + hl
                            g.wait_ge(sYD.h, 16 * (hg + 1))
                            sAGY[hg].inc(g.collective_compute(
                                "AllGather", ALU.bypass, replica_groups=RG_HALF,
                                ins=[yin[hg].ap().opt()], outs=[yout[hg].ap().opt()]))

                    @blk.tensor
                    def _(t):
                        t.wait_ge(sCONST.h, 48)
                        t.wait_ge(sWQ.h, 16 * (p + 1))
                        for sp in range(3):
                            if p + sp > 0:
                                t.wait_ge(sWS.h, 3 * p + sp)
                            for dt in range(DT):
                                ii = t.matmul(ps_bc[0:1, 0:256], onesb[:, 0:1],
                                              wqkv_sb[:, dt * 768 + sp * 256:dt * 768 + (sp + 1) * 256],
                                              start=(dt == 0), stop=(dt == DT - 1))
                            sWST.inc(ii)
                        t.wait_ge(sWS.h, 3 * (p + 1))

                        def emit_stats(c):
                            k = C * p + c
                            t.wait_ge(sXT[k % 2].h, 16 * (k // 2 + 1))
                            if c >= 1:
                                t.wait_ge(sROW1.h, c)
                            for dt in range(DT):
                                t.matmul(ps_stats[0:1, 0:512], onesb[:, 0:1],
                                         xTb[:, (c % 2) * DT * 512 + dt * 512:(c % 2) * DT * 512 + (dt + 1) * 512],
                                         start=(dt == 0), stop=(dt == DT - 1))
                            for half in range(2):
                                t.wait_ge(sSQ.h, 2 * c + half + 1)
                                for dt in range(DT // 2):
                                    ii = t.matmul(ps_stats[64:65, 0:512], onesb[:, 0:1],
                                                  xsqb[:, dt * 512:(dt + 1) * 512],
                                                  start=(half == 0 and dt == 0),
                                                  stop=(half == 1 and dt == DT // 2 - 1))
                                if half == 0:
                                    sSQ2.inc(ii)
                            sSTAT.inc(ii)

                        if p == 0:
                            emit_stats(0)
                        for c in range(C):
                            k = C * p + c
                            t.wait_ge(sXT[k % 2].h, 16 * (k // 2 + 1))
                            if p == 0:
                                t.wait_ge(sROW2.h, 2 * (c + 1))
                            t.wait_ge(sBCE.h, C * p + c)
                            sBC.inc(t.matmul(ps_bc[:, 0:512], onesb[0:1, :],
                                             rstd_row[0:1, c * 512:(c + 1) * 512],
                                             start=True, stop=True))
                            if p == 0:
                                for j in range(4):
                                    if j == 0:
                                        t.wait_ge(sBCE.h, c + 1)
                                    t.wait_ge(sRTE.h, 4 * c + j)
                                    sRT.inc(t.transpose(
                                        ps_rt[:],
                                        rbc[:, (c % 2) * 512 + j * 128:(c % 2) * 512 + (j + 1) * 128],
                                        idb[:]))
                                if c + 1 < C:
                                    emit_stats(c + 1)
                            for ft in range(4):
                                gq = 4 * k + ft
                                if gq >= 2:
                                    t.wait_ge(sPRE.h, gq - 1)
                                for dt in range(DT):
                                    t.matmul(ps_qk[gq % 2][:],
                                             wqkv_sb[:, dt * 768 + ft * 128:dt * 768 + (ft + 1) * 128],
                                             xTb[:, (c % 2) * DT * 512 + dt * 512:(c % 2) * DT * 512 + (dt + 1) * 512],
                                             start=(dt == 0), stop=False)
                                off = ft * 128 if ft < 2 else 256 + (ft - 2) * 128
                                sQK.inc(t.matmul(ps_qk[gq % 2][:],
                                                 ws_neg[0:1, off:off + 128],
                                                 m_row[0:1, c * 512:(c + 1) * 512],
                                                 start=False, stop=True))
                            for kt in range(4):
                                gv = 4 * k + kt
                                if gv >= 2:
                                    t.wait_ge(sVEV.h, gv - 1)
                                for dt in range(DT):
                                    t.matmul(ps_v[gv % 2][:, 0:256],
                                             xTb[:, (c % 2) * DT * 512 + dt * 512 + kt * 128:(c % 2) * DT * 512 + dt * 512 + (kt + 1) * 128],
                                             wqkv_sb[:, dt * 768 + 512:dt * 768 + 768],
                                             start=(dt == 0), stop=False)
                                sVM.inc(t.matmul(ps_v[gv % 2][:, 0:256],
                                                 m_row[0:1, c * 512 + kt * 128:c * 512 + (kt + 1) * 128],
                                                 ws_neg[0:1, 512:768],
                                                 start=False, stop=True))
                        t.wait_ge(sROPE.h, 4 * C * (p + 1))
                        t.wait_ge(sVEV.h, 4 * C * (p + 1))
                        t.wait_ge(sVN1.h, 1)
                        if p == 0:
                            t.wait_ge(sRTE.h, 4 * C)
                        t.wait_ge(sBCE.h, C * (p + 1))

                    ps_st = ps_qk
                    ps_pv = [ps_v[0], ps_v[1], ps_stats, ps_bc]
                    ps_yt = [ps_rt, ps_yt1]

                    def d_decomp(dg):
                        hgp = dg // (4 * C)
                        r = dg % (4 * C)
                        return hgp, r // 4, r % 4

                    def tile_idx(hg, qc, kb):
                        return hg * Mh + 2 * qc * (qc + 1) + kb

                    @blk.tensor
                    def _(t):
                        if ATTN == 0:
                            return
                        for hl in range(2):
                            hg = 2 * p + hl

                            def emit_ytr(qcq):
                                for j in range(4):
                                    g4 = hg * 4 * C + 4 * qcq + j
                                    t.wait_ge(sYEV.h, g4 + 1)
                                    if ATTN >= 5 and g4 >= 2:
                                        t.wait_ge(sYTS.h, g4 - 1)
                                    sYTR.inc(t.transpose(
                                        ps_yt[g4 % 2][:],
                                        ynat[:, j * 128:(j + 1) * 128], idb[:]))

                            def emit_pv(qc, kb):
                                i = tile_idx(hg, qc, kb)
                                gq = hg * C + qc
                                diag = kb >= 4 * qc
                                if diag:
                                    dg = hg * 4 * C + 4 * qc + (kb - 4 * qc)
                                    t.wait_ge(sMSK.h, dg + 1)
                                    src = EbufM[:, (dg % 2) * 512:(dg % 2 + 1) * 512]
                                else:
                                    t.wait_ge(sEXP.h, i + 1)
                                    src = Ebuf[:, (i % 3) * 512:(i % 3 + 1) * 512]
                                if kb == 0 and gq >= 2:
                                    t.wait_ge(sYEV.h, 4 * (gq - 1))
                                for jq in range(4):
                                    pv = ps_pv[(qc % 2) * 2 + jq // 2]
                                    ii = t.matmul(
                                        pv[:, (jq % 2) * 130:(jq % 2) * 130 + 129],
                                        src[:, jq * 128:(jq + 1) * 128],
                                        Vn[:, hl * KT * 132 + kb * 132:hl * KT * 132 + kb * 132 + 129],
                                        start=(kb == 0 and jq % 2 == 0),
                                        stop=(kb == 4 * qc + 3 and jq % 2 == 1))
                                    if jq == 3:
                                        sPVT.inc(ii)

                            pend = None
                            for qc in range(C):
                                for kb in range(4 * qc + 4):
                                    i = tile_idx(hg, qc, kb)
                                    if i >= 2:
                                        t.wait_ge(sEXP.h, i - 1)
                                    sST.inc(t.matmul(
                                        ps_st[i % 2][:],
                                        KTs[:, hl * N + kb * 128:hl * N + (kb + 1) * 128],
                                        QTs[:, hl * N + qc * 512:hl * N + (qc + 1) * 512],
                                        start=True, stop=True))
                                    if ATTN >= 3 and pend is not None:
                                        emit_pv(*pend)
                                        if pend[1] == 4 * pend[0] + 3:
                                            if ATTN >= 4 and pend[0] >= 1:
                                                emit_ytr(pend[0] - 1)
                                    pend = (qc, kb)
                            if ATTN >= 3 and pend is not None:
                                emit_pv(*pend)
                                if ATTN >= 4:
                                    if C >= 2:
                                        emit_ytr(C - 2)
                                    emit_ytr(C - 1)

                    @blk.scalar
                    def _(sc):
                        for sp in range(3):
                            sc.wait_ge(sWST.h, 3 * p + sp + 1)
                            sWS.inc(sc.activation(ws_neg[0:1, sp * 256:(sp + 1) * 256],
                                                  ps_bc[0:1, 0:256], AF.Copy, scale=-1.0))
                        def emit_squares(c, half):
                            if half == 0:
                                sc.wait_ge(sXT[c % 2].h, 16 * (c // 2 + 1))
                                if c >= 1:
                                    sc.wait_ge(sSTAT.h, c)
                            else:
                                sc.wait_ge(sSQ2.h, c + 1)
                            for dt in range(DT // 2):
                                dg = half * (DT // 2) + dt
                                ii = sc.activation(
                                    xsqb[:, dt * 512:(dt + 1) * 512],
                                    xTb[:, (c % 2) * DT * 512 + dg * 512:(c % 2) * DT * 512 + (dg + 1) * 512],
                                    AF.Square)
                            sSQ.inc(ii)

                        def srows(j):
                            sc.wait_ge(sROW1.h, j + 1)
                            if j >= 1:
                                sc.wait_ge(sROW2.h, 2 * j)
                            sSS.inc(sc.activation(varf[0:1, :], lnvf[0:1, :], AF.Ln))
                            sc.wait_ge(sSS.h, sSS.n)
                            sROWS.inc(sc.activation(rstdf[0:1, :], varf[0:1, :],
                                                    AF.Exp, scale=-0.5))

                        if p == 0:
                            emit_squares(0, 0)
                            emit_squares(0, 1)
                            srows(0)
                        for c in range(C):
                            k = C * p + c
                            sc.wait_ge(sBC.h, C * p + c + 1)
                            if C * p + c >= 2:
                                sc.wait_ge(sCSR.h, C * p + c - 1)
                                if p == 0 and c >= 2:
                                    sc.wait_ge(sRT.h, 4 * (c - 1))
                            sBCE.inc(sc.activation(
                                rbc[:, (c % 2) * 512:(c % 2 + 1) * 512],
                                ps_bc[:, 0:512], AF.Copy))
                            if p == 0:
                                for j in range(4):
                                    sc.wait_ge(sRT.h, 4 * c + j + 1)
                                    sRTE.inc(sc.activation(
                                        rstd_col[:, 4 * c + j:4 * c + j + 1],
                                        ps_rt[:, 0:1], AF.Copy))
                                if c + 1 < C:
                                    emit_squares(c + 1, 0)
                                    emit_squares(c + 1, 1)
                                    srows(c + 1)
                            for ft in range(4):
                                gq = 4 * k + ft
                                sc.wait_ge(sQK.h, gq + 1)
                                if gq >= 2:
                                    sc.wait_ge(sROPE.h, gq - 1)
                                sl = slice((gq % 2) * 512, (gq % 2 + 1) * 512)
                                sc.activation(prer_e[0:64, sl], ps_qk[gq % 2][0:64, :], AF.Copy)
                                sc.activation(prer_e[64:128, sl], ps_qk[gq % 2][0:64, :], AF.Copy)
                                sc.activation(prer_o[0:64, sl], ps_qk[gq % 2][64:128, :], AF.Copy)
                                sPRE.inc(sc.activation(prer_o[64:128, sl],
                                                       ps_qk[gq % 2][64:128, :], AF.Copy))
                            for kt in range(4):
                                gv = 4 * k + kt
                                sc.wait_ge(sVM.h, gv + 1)
                                ktg = 4 * c + kt
                                if p == 0:
                                    sc.wait_ge(sRTE.h, ktg + 1)
                                vview = Vn.ap().rearrange(
                                    "p (h k w) -> p h k w", h=2, k=KT)[:, :, ktg, 0:128]
                                sVEV.inc(sc.activation(
                                    vview, ps_v[gv % 2][:, 0:256], AF.Copy,
                                    scale=rstd_col[:, ktg:ktg + 1]))
                        if ATTN == 0:
                            return
                        for hl in range(2):
                            hg = 2 * p + hl

                            def emit_yts1(qcq, jq):
                                g4 = hg * 4 * C + 4 * qcq + jq
                                if ATTN >= 5 and qcq == 0 and jq == 0:
                                    sc.wait_ge(sYD.h, 16 * hg)
                                sc.wait_ge(sYTR.h, g4 + 1)
                                sYTS.inc(sc.activation(
                                    yT_send[:, (qcq * 4 + jq) * 128:(qcq * 4 + jq + 1) * 128],
                                    ps_yt[g4 % 2][:],
                                    AF.Copy))

                            for qc in range(C):
                                gq = hg * C + qc
                                for kb in range(4 * qc + 4):
                                    i = tile_idx(hg, qc, kb)
                                    sc.wait_ge(sST.h, i + 1)
                                    if ATTN >= 3 and i >= 3:
                                        sc.wait_ge(sPVT.h, i - 2)
                                    sEXP.inc(sc.activation(
                                        Ebuf[:, (i % 3) * 512:(i % 3 + 1) * 512],
                                        ps_st[i % 2][:], AF.Exp))
                                for jq in range(4):
                                    if ATTN < 3:
                                        continue
                                    g4 = hg * 4 * C + 4 * qc + jq
                                    sc.wait_ge(sREC.h, g4 + 1)
                                    if ATTN >= 4 and g4 >= 4:
                                        sc.wait_ge(sYTR.h, g4 - 3)
                                    pv = ps_pv[(qc % 2) * 2 + jq // 2]
                                    sYEV.inc(sc.activation(
                                        ynat[:, jq * 128:(jq + 1) * 128],
                                        pv[:, (jq % 2) * 130:(jq % 2) * 130 + 128],
                                        AF.Copy, scale=rec[:, jq:jq + 1]))
                                    if ATTN >= 5 and qc >= 1:
                                        emit_yts1(qc - 1, jq)
                            if ATTN >= 5:
                                for jq in range(4):
                                    emit_yts1(C - 1, jq)

                    @blk.vector
                    def _(v):
                        v.wait_ge(sCM.h, 48)
                        def vrows(j):
                            v.wait_ge(sSTAT.h, j + 1)
                            if j >= 1:
                                v.wait_ge(sROW2.h, 2 * j)
                                v.wait_ge(sROWS.h, j)
                            sVT.inc(v.tensor_scalar_mul(m_f32[0:1, :],
                                                        ps_stats[0:1, 0:512], 1.0 / D))
                            sVT.inc(v.tensor_scalar_mul(s2f[0:1, :],
                                                        ps_stats[64:65, 0:512], 1.0 / D))
                            v.wait_ge(sVT.h, sVT.n)
                            sVT.inc(v.tensor_mul(m2f[0:1, :], m_f32[0:1, :],
                                                 m_f32[0:1, :]))
                            v.wait_ge(sVT.h, sVT.n)
                            sVT.inc(v.tensor_sub(varf[0:1, :], s2f[0:1, :],
                                                 m2f[0:1, :]))
                            v.wait_ge(sVT.h, sVT.n)
                            sROW1.inc(v.tensor_scalar_add(lnvf[0:1, :],
                                                          varf[0:1, :], EPS_LN))
                            v.wait_ge(sROWS.h, j + 1)
                            sROW2.inc(v.tensor_copy(m_row[0:1, j * 512:(j + 1) * 512],
                                                    m_f32[0:1, :]))
                            sROW2.inc(v.tensor_copy(
                                rstd_row[0:1, j * 512:(j + 1) * 512], rstdf[0:1, :]))

                        if p == 0:
                            sVN1.inc(v.memset(Vn.ap().rearrange(
                                "p (h k w) -> p h k w", h=2, k=KT)[:, :, :, 128:129], 1.0))
                            vrows(0)
                        for c in range(C):
                            k = C * p + c
                            v.wait_ge(sBCE.h, C * p + c + 1)
                            if k >= 1:
                                v.wait_ge(sROPE.h, 4 * k)
                            i1 = v.tensor_mul(cosr[:],
                                              cs_sb[:, c * 512:(c + 1) * 512],
                                              rbc[:, (c % 2) * 512:(c % 2 + 1) * 512])
                            sVT.inc(i1)
                            csr_done = sVT.n
                            i2 = v.tensor_mul(sinr[:],
                                              cs2_sb[:, c * 512:(c + 1) * 512],
                                              rbc[:, (c % 2) * 512:(c % 2 + 1) * 512])
                            sCSR.inc(i2)
                            if p == 0 and c + 1 < C:
                                vrows(c + 1)
                            for ft in range(4):
                                gq = 4 * k + ft
                                v.wait_ge(sPRE.h, gq + 1)
                                v.wait_ge(sVT.h, csr_done)
                                v.wait_ge(sCSR.h, C * p + c + 1)
                                sl = slice((gq % 2) * 512, (gq % 2 + 1) * 512)
                                if ft < 2:
                                    dst = QTs[:, ft * N + c * 512:ft * N + (c + 1) * 512]
                                else:
                                    dst = KTs[:, (ft - 2) * N + c * 512:(ft - 2) * N + (c + 1) * 512]
                                if sROPE.n > 0:
                                    v.wait_ge(sROPE.h, sROPE.n)
                                sVT.inc(v.tensor_mul(rt1[:], prer_e[:, sl], cosr[:]))
                                sVT.inc(v.tensor_mul(rt2[:], prer_o[:, sl], sinr[:]))
                                v.wait_ge(sVT.h, sVT.n)
                                sROPE.inc(v.tensor_add(dst, rt1[:], rt2[:]))
                        if ATTN < 3:
                            return
                        for hl in range(2):
                            hg = 2 * p + hl
                            for qc in range(C):
                                for j in range(4):
                                    kb = 4 * qc + j
                                    i = tile_idx(hg, qc, kb)
                                    dg = hg * 4 * C + 4 * qc + j
                                    v.wait_ge(sEXP.h, i + 1)
                                    if dg >= 2:
                                        hgp, qcp, jp = d_decomp(dg - 2)
                                        v.wait_ge(sPVT.h,
                                                  tile_idx(hgp, qcp, 4 * qcp + jp) + 1)
                                    sMSK.inc(v.tensor_mul(
                                        EbufM[:, (dg % 2) * 512:(dg % 2 + 1) * 512],
                                        Ebuf[:, (i % 3) * 512:(i % 3 + 1) * 512],
                                        masks[:, j * 512:(j + 1) * 512]))
                                ilast = tile_idx(hg, qc, 4 * qc + 3)
                                v.wait_ge(sPVT.h, ilast + 1)
                                for jq in range(4):
                                    g4 = hg * 4 * C + 4 * qc + jq
                                    if g4 >= 4:
                                        v.wait_ge(sYEV.h, g4 - 3)
                                    pv = ps_pv[(qc % 2) * 2 + jq // 2]
                                    sREC.inc(v.reciprocal(
                                        rec[:, jq:jq + 1],
                                        pv[:, (jq % 2) * 130 + 128:(jq % 2) * 130 + 129]))

                    @blk.sync
                    def _(s):
                        if ATTN < 5:
                            return
                        for hl in range(2):
                            hg = 2 * p + hl
                            s.wait_ge(sYTS.h, 4 * C * (hg + 1))
                            sYD.inc(s.dma_start(yin[hg].ap(), yT_send[:]), 16)

            if PHASES >= 2:
                emit_pass(0)
            if PHASES >= 3:
                emit_pass(1)

        with ExitStack() as wfctx:
            if PHASES < 4:
                return nc
            s_nat = wfctx.enter_context(nc.sbuf_tensor("s_nat", [128, TT_ * D], BF16))
            gtile = wfctx.enter_context(nc.sbuf_tensor("gtile", [128, 2 * 512], BF16))

            with ExitStack() as fctx:
                xnT = fctx.enter_context(nc.sbuf_tensor("xnT", [128, DT * Nq], BF16))

                with ExitStack() as wctx:
                    yT_own = wctx.enter_context(nc.sbuf_tensor("yT_own", [128, 16 * Nq], BF16))
                    wo_buf = wctx.enter_context(nc.sbuf_tensor("wo_buf", [128, 4 * 512], BF16))
                    xob = wctx.enter_context(nc.sbuf_tensor("xob", [128, DT * 512], BF16))
                    xsqF = wctx.enter_context(nc.sbuf_tensor("xsqF", [128, DT * 512], BF16))
                    rbF = wctx.enter_context(nc.sbuf_tensor("rbF", [128, 512], BF16))
                    mrbF = wctx.enter_context(nc.sbuf_tensor("mrbF", [128, 512], F32))
                    tmpF = wctx.enter_context(nc.sbuf_tensor("tmpF", [128, 512], F32))
                    mF = wctx.enter_context(nc.sbuf_tensor("mF", [1, 512], F32))
                    s2F = wctx.enter_context(nc.sbuf_tensor("s2F", [1, 512], F32))
                    m2F = wctx.enter_context(nc.sbuf_tensor("m2F", [1, 512], F32))
                    varF = wctx.enter_context(nc.sbuf_tensor("varF", [1, 512], F32))
                    lnvF = wctx.enter_context(nc.sbuf_tensor("lnvF", [1, 512], F32))
                    rstdF = wctx.enter_context(nc.sbuf_tensor("rstdF", [1, 512], F32))
                    mrsF = wctx.enter_context(nc.sbuf_tensor("mrsF", [1, 512], F32))
                    blkW = wctx.enter_context(nc.Block())
                    blkF1 = blkW
                    wp = ExitStack()
                    ps_w = [wp.enter_context(nc.psum_tensor(f"ps_w{i}", [128, 512], F32))
                            for i in range(TT_)]
                    @blkW.gpsimd
                    def _(g):
                        def emit_gather(i):
                            g.wait_ge(sAGY[i].h, 1)
                            for j in range(4):
                                sYO[i].inc(g.indirect_dma_start(
                                    yT_own[:, (i * 4 + j) * Nq:(i * 4 + j + 1) * Nq], None,
                                    yout[i].ap(),
                                    IndirectOffsetOnAxis(yix[:, j:j + 1], 1)), 16)

                        def emit_wo(k0, k1):
                            for kk in range(k0, k1):
                                oc, ft = kk // 16, kk % 16
                                if kk >= 4:
                                    g.wait_ge(sWOT.h, kk - 3)
                                sWO[kk % 4].inc(g.dma_start(
                                    wo_buf[:, (kk % 4) * 512:(kk % 4 + 1) * 512],
                                    wo[ft * 128:(ft + 1) * 128, oc * 512:(oc + 1) * 512]), 16)

                        for qi_ in range(4):
                            g.wait_ge(sQO[qi_].h, 16 * ((NG - qi_ + 3) // 4))
                        sAGW[2].inc(g.collective_compute(
                            "AllGather", ALU.bypass, replica_groups=RG_ALL,
                            ins=[gam_specs[2][4].ap().opt()],
                            outs=[gam_specs[2][5].ap().opt()]))
                        emit_gather(0)
                        emit_wo(0, 4)
                        emit_gather(1)
                        emit_wo(4, 8)
                        emit_gather(2)
                        emit_wo(8, 12)
                        emit_gather(3)
                        emit_wo(12, 64)

                    @blkW.tensor
                    def _(t):
                        for oc in range(4):
                            if oc >= 1:
                                t.wait_ge(sSEV.h, TT_ * oc)
                            for ft in range(16):
                                kk = oc * 16 + ft
                                if oc == 0 and ft % 4 == 0:
                                    t.wait_ge(sYO[ft // 4].h, 64)
                                t.wait_ge(sWO[kk % 4].h, 16 * (kk // 4 + 1))
                                for tt in range(TT_):
                                    ii = t.matmul(
                                        ps_w[tt][:],
                                        yT_own[:, ft * Nq + tt * 128:ft * Nq + (tt + 1) * 128],
                                        wo_buf[:, (kk % 4) * 512:(kk % 4 + 1) * 512],
                                        start=(ft == 0), stop=(ft == 15))
                                sWOT.inc(ii)

                    @blkW.scalar
                    def _(sc):
                        for oc in range(4):
                            sc.wait_ge(sWOT.h, 16 * (oc + 1))
                            for tt in range(TT_):
                                sSEV.inc(sc.activation(
                                    s_nat[:, tt * D + oc * 512:tt * D + (oc + 1) * 512],
                                    ps_w[tt][:], AF.Copy))

                    wp.close()
                    ps_fs = wctx.enter_context(nc.psum_tensor("ps_fs", [128, 512], F32))
                    ps_fq = wctx.enter_context(nc.psum_tensor("ps_fq", [128, 512], F32))
                    ps_fb = wctx.enter_context(nc.psum_tensor("ps_fb", [128, 512], F32))
                    if PHASES >= 5:
                        @blkF1.gpsimd
                        def _(g):
                            for tc in range(TC):
                                if tc >= 1:
                                    g.wait_ge(sXN.h, DT * tc)
                                sXO.inc(g.dma_start(
                                    xob[:],
                                    x_ownT[tc * 128:(tc + 1) * 128, :]),
                                    16)

                        @blkF1.tensor
                        def _(t):
                            t.wait_ge(sSEV.h, 4 * TT_)
                            for tc in range(TC):
                                t.wait_ge(sXO.h, 16 * (tc + 1))
                                if tc >= 1:
                                    t.wait_ge(sFR1.h, 2 * tc)
                                for dt in range(DT):
                                    t.matmul(ps_fs[0:1, 0:512], onesb[:, 0:1],
                                             xob[:, dt * 512:(dt + 1) * 512],
                                             start=(dt == 0), stop=(dt == DT - 1))
                                t.wait_ge(sFSQ.h, tc + 1)
                                for dt in range(DT):
                                    ii = t.matmul(ps_fq[0:1, 0:512], onesb[:, 0:1],
                                                  xsqF[:, dt * 512:(dt + 1) * 512],
                                                  start=(dt == 0), stop=(dt == DT - 1))
                                sFST.inc(ii)
                                t.wait_ge(sFRS.h, tc + 1)
                                if tc >= 1:
                                    t.wait_ge(sFBE.h, 2 * tc)
                                sFBC.inc(t.matmul(ps_fb[:, 0:512], onesf[0:1, :],
                                                  rstdF[0:1, :], start=True, stop=True))
                                t.wait_ge(sFR1.h, 2 * tc + 2)
                                t.wait_ge(sFBE.h, 2 * tc + 1)
                                sFBC.inc(t.matmul(ps_fb[:, 0:512], onesf[0:1, :],
                                                  mrsF[0:1, :], start=True, stop=True))

                        @blkF1.scalar
                        def _(sc):
                            for tc in range(TC):
                                sc.wait_ge(sXO.h, 16 * (tc + 1))
                                if tc >= 1:
                                    sc.wait_ge(sFST.h, tc)
                                for dt in range(DT):
                                    ii = sc.activation(xsqF[:, dt * 512:(dt + 1) * 512],
                                                       xob[:, dt * 512:(dt + 1) * 512],
                                                       AF.Square)
                                sFSQ.inc(ii)
                                sc.wait_ge(sFR1.h, 2 * tc + 1)
                                if tc >= 1:
                                    sc.wait_ge(sXN.h, DT * tc)
                                sSS.inc(sc.activation(varF[0:1, :], lnvF[0:1, :], AF.Ln))
                                sc.wait_ge(sSS.h, sSS.n)
                                sFRS.inc(sc.activation(rstdF[0:1, :], varF[0:1, :], AF.Exp,
                                                       scale=-0.5))
                                sc.wait_ge(sFBC.h, 2 * tc + 1)
                                sFBE.inc(sc.activation(rbF[:], ps_fb[:, 0:512], AF.Copy))
                                sc.wait_ge(sFBC.h, 2 * tc + 2)
                                sFBE.inc(sc.activation(mrbF[:], ps_fb[:, 0:512], AF.Copy))

                        @blkF1.vector
                        def _(v):
                            for tc in range(TC):
                                v.wait_ge(sFST.h, tc + 1)
                                if tc >= 1:
                                    v.wait_ge(sFR1.h, 2 * tc)
                                    v.wait_ge(sFRS.h, tc)
                                sVT.inc(v.tensor_scalar_mul(mF[0:1, :], ps_fs[0:1, 0:512],
                                                            1.0 / D))
                                sVT.inc(v.tensor_scalar_mul(s2F[0:1, :], ps_fq[0:1, 0:512],
                                                            1.0 / D))
                                v.wait_ge(sVT.h, sVT.n)
                                sVT.inc(v.tensor_mul(m2F[0:1, :], mF[0:1, :], mF[0:1, :]))
                                v.wait_ge(sVT.h, sVT.n)
                                sVT.inc(v.tensor_sub(varF[0:1, :], s2F[0:1, :], m2F[0:1, :]))
                                v.wait_ge(sVT.h, sVT.n)
                                sFR1.inc(v.tensor_scalar_add(lnvF[0:1, :], varF[0:1, :],
                                                             EPS_LN))
                                v.wait_ge(sFRS.h, tc + 1)
                                sFR1.inc(v.tensor_mul(mrsF[0:1, :], mF[0:1, :], rstdF[0:1, :]))
                                v.wait_ge(sFBE.h, 2 * tc + 2)
                                for dt in range(DT):
                                    if sXN.n > 0:
                                        v.wait_ge(sXN.h, sXN.n)
                                    sVT.inc(v.tensor_mul(tmpF[:],
                                                         xob[:, dt * 512:(dt + 1) * 512],
                                                         rbF[:]))
                                    v.wait_ge(sVT.h, sVT.n)
                                    sXN.inc(v.tensor_sub(
                                        xnT[:, dt * Nq + tc * 512:dt * Nq + (tc + 1) * 512],
                                        tmpF[:], mrbF[:]))

                if PHASES < 5:
                    return nc
                f23ctx = fctx.enter_context(ExitStack())
                hidT = f23ctx.enter_context(nc.sbuf_tensor("hidT", [128, HT * Nq], BF16))
                with ExitStack() as f2ctx:
                    w1tb = f2ctx.enter_context(nc.sbuf_tensor("w1tb", [128, 2 * 2048], BF16))
                    w2tb = f2ctx.enter_context(nc.sbuf_tensor("w2tb", [128, 2 * 2048], BF16))
                    sgb = f2ctx.enter_context(nc.sbuf_tensor("sgb", [128, 2 * Nq], BF16))
                    x2b = f2ctx.enter_context(nc.sbuf_tensor("x2b", [128, 2 * Nq], BF16))
                    ps_f = [f2ctx.enter_context(nc.psum_tensor(f"ps_f{i}", [128, 512], F32))
                            for i in range(2 * 2 * TC)]
                    blkF2 = f2ctx.enter_context(nc.Block())

                    @blkF2.sync
                    def _(s):
                        s.wait_ge(sAGW[0].h, 1)
                        s.wait_ge(sAGW[1].h, 1)
                        for ht in range(HT):
                            if ht >= 2:
                                s.wait_ge(sFM.h, 2 * TC * (ht - 1))
                            sW12[ht % 2].inc(s.dma_start(
                                w1tb[:, (ht % 2) * 2048:(ht % 2 + 1) * 2048],
                                w1qo.rearrange("(c p) (h d x) -> p c h d x",
                                               c=8, p=128, d=2, x=128)[:, :, ht, :, :]), 16)
                            sW12[ht % 2].inc(s.dma_start(
                                w2tb[:, (ht % 2) * 2048:(ht % 2 + 1) * 2048],
                                w2qo.rearrange("(c p) (h d x) -> p c h d x",
                                               c=8, p=128, d=2, x=128)[:, :, ht, :, :]), 16)

                    @blkF2.tensor
                    def _(t):
                        t.wait_ge(sXN.h, DT * TC)
                        for ht in range(HT):
                            t.wait_ge(sW12[ht % 2].h, 32 * (ht // 2 + 1))
                            if ht >= 2:
                                t.wait_ge(sFE.h, 2 * TC * (ht - 1))
                            for mi, wtb in enumerate((w1tb, w2tb)):
                                for dt in range(DT):
                                    lhs = wtb[:, (ht % 2) * 2048 + dt * 128:(ht % 2) * 2048 + (dt + 1) * 128]
                                    for tc in range(TC):
                                        ii = t.matmul(
                                            ps_f[(ht % 2) * 2 * TC + mi * TC + tc][:],
                                            lhs,
                                            xnT[:, dt * Nq + tc * 512:dt * Nq + (tc + 1) * 512],
                                            start=(dt == 0), stop=(dt == DT - 1))
                                        if dt == DT - 1:
                                            sFM.inc(ii)

                    @blkF2.scalar
                    def _(sc):
                        for ht in range(HT):
                            if ht >= 2:
                                sc.wait_ge(sHID.h, TC * (ht - 1))
                            for tc in range(TC):
                                sc.wait_ge(sFM.h, 2 * TC * ht + tc + 1)
                                sFE.inc(sc.activation(
                                    sgb[:, (ht % 2) * Nq + tc * 512:(ht % 2) * Nq + (tc + 1) * 512],
                                    ps_f[(ht % 2) * 2 * TC + 0 * TC + tc][:], AF.Sigmoid))
                            for tc in range(TC):
                                sc.wait_ge(sFM.h, 2 * TC * ht + TC + tc + 1)
                                sFE.inc(sc.activation(
                                    x2b[:, (ht % 2) * Nq + tc * 512:(ht % 2) * Nq + (tc + 1) * 512],
                                    ps_f[(ht % 2) * 2 * TC + 1 * TC + tc][:], AF.Copy))

                    @blkF2.vector
                    def _(v):
                        for ht in range(HT):
                            v.wait_ge(sFE.h, 2 * TC * (ht + 1))
                            for tc in range(TC):
                                sHID.inc(v.tensor_mul(
                                    hidT[:, ht * Nq + tc * 512:ht * Nq + (tc + 1) * 512],
                                    sgb[:, (ht % 2) * Nq + tc * 512:(ht % 2) * Nq + (tc + 1) * 512],
                                    x2b[:, (ht % 2) * Nq + tc * 512:(ht % 2) * Nq + (tc + 1) * 512]))

                with ExitStack() as f3ctx:
                    w3rb = f3ctx.enter_context(nc.sbuf_tensor("w3rb", [128, 4 * 512], BF16))
                    pbuf = f3ctx.enter_context(nc.sbuf_tensor("pbuf", [128, 512], BF16))
                    xbuf = f3ctx.enter_context(nc.sbuf_tensor("xbuf", [128, 2 * D], F32))
                    sgf = f3ctx.enter_context(nc.sbuf_tensor("sgf", [128, D], F32))
                    obuf = f3ctx.enter_context(nc.sbuf_tensor("obuf", [128, 2 * D], F32))
                    ps_g3 = [f3ctx.enter_context(nc.psum_tensor(f"ps_g3{i}", [128, 512], F32))
                             for i in range(TT_)]
                    blkF3 = f3ctx.enter_context(nc.Block())

                    @blkF3.sync
                    def _(s):
                        s.wait_ge(sAGW[2].h, 1)
                        for tt in range(2):
                            sXL[tt % 2].inc(s.dma_start(
                                xbuf[:, (tt % 2) * D:(tt % 2 + 1) * D],
                                x_own[tt * 128:(tt + 1) * 128, :]), 16)
                        for og in range(4):
                            for ht in range(HT):
                                kk = og * HT + ht
                                if kk >= 4:
                                    s.wait_ge(sW3T.h, kk - 3)
                                sW3L[kk % 4].inc(s.dma_start(
                                    w3rb[:, (kk % 4) * 512:(kk % 4 + 1) * 512],
                                    w3qo.rearrange("(g p) (h x) -> p g h x",
                                                   g=8, p=128, x=256)[:, 2 * og:2 * og + 2, ht, :]),
                                    16)
                        for tt in range(2, TT_ + 2):
                            if tt < TT_:
                                s.wait_ge(sVF.h, tt - 1)
                                sXL[tt % 2].inc(s.dma_start(
                                    xbuf[:, (tt % 2) * D:(tt % 2 + 1) * D],
                                    x_own[tt * 128:(tt + 1) * 128, :]), 16)
                            ts_ = tt - 2
                            s.wait_ge(sVF.h, ts_ + 1)
                            sOUT[ts_ % 2].inc(s.dma_start(
                                out[ts_ * 128:(ts_ + 1) * 128, :],
                                obuf[:, (ts_ % 2) * D:(ts_ % 2 + 1) * D]), 16)
                        s.wait_ge(sOUT[0].h, 16 * ((TT_ + 1) // 2))
                        s.wait_ge(sOUT[1].h, 16 * (TT_ // 2))

                    @blkF3.tensor
                    def _(t):
                        t.wait_ge(sHID.h, TC * HT)
                        for og in range(4):
                            if og >= 1:
                                t.wait_ge(sGE.h, TT_ * og)
                            for ht in range(HT):
                                kk = og * HT + ht
                                t.wait_ge(sW3L[kk % 4].h, 16 * (kk // 4 + 1))
                                for t8 in range(TT_):
                                    ii = t.matmul(
                                        ps_g3[t8][:],
                                        hidT[:, ht * Nq + t8 * 128:ht * Nq + (t8 + 1) * 128],
                                        w3rb[:, (kk % 4) * 512:(kk % 4 + 1) * 512],
                                        start=(ht == 0), stop=(ht == HT - 1))
                                sW3T.inc(ii)

                    @blkF3.scalar
                    def _(sc):
                        for og in range(4):
                            sc.wait_ge(sW3T.h, HT * (og + 1))
                            for t8 in range(TT_):
                                q = og * TT_ + t8
                                if q >= 2:
                                    sc.wait_ge(sSG.h, q - 1)
                                sGE.inc(sc.activation(
                                    gtile[:, (q % 2) * 512:(q % 2 + 1) * 512],
                                    ps_g3[t8][:], AF.Silu))

                    @blkF3.vector
                    def _(v):
                        for og in range(4):
                            for t8 in range(TT_):
                                q = og * TT_ + t8
                                v.wait_ge(sGE.h, q + 1)
                                if q >= 1:
                                    v.wait_ge(sSG.h, q)
                                sVT.inc(v.tensor_mul(
                                    pbuf[:],
                                    gtile[:, (q % 2) * 512:(q % 2 + 1) * 512],
                                    s_nat[:, t8 * D + og * 512:t8 * D + (og + 1) * 512]))
                                v.wait_ge(sVT.h, sVT.n)
                                sSG.inc(v.tensor_copy(
                                    s_nat[:, t8 * D + og * 512:t8 * D + (og + 1) * 512],
                                    pbuf[:]))
                                if og == 3:
                                    tt = t8
                                    v.wait_ge(sSG.h, sSG.n)
                                    v.wait_ge(sXL[tt % 2].h, 16 * (tt // 2 + 1))
                                    if tt >= 2:
                                        v.wait_ge(sOUT[tt % 2].h,
                                                  16 * ((tt - 2) // 2 + 1))
                                    sVT.inc(v.tensor_copy(
                                        sgf[:], s_nat[:, tt * D:(tt + 1) * D]))
                                    v.wait_ge(sVT.h, sVT.n)
                                    sVF.inc(v.tensor_add(
                                        obuf[:, (tt % 2) * D:(tt % 2 + 1) * D],
                                        xbuf[:, (tt % 2) * D:(tt % 2 + 1) * D],
                                        sgf[:]))

    return nc


def _rope_perm():
    p = np.zeros(128, dtype=np.int64)
    p[0:64] = np.arange(0, 128, 2)
    p[64:128] = np.arange(1, 128, 2)
    return p


def prep_inputs(x, W_qkv, W_o, w1, w2, w3, N, H):
    Nq = N // 4
    HT = H // 128
    perm = _rope_perm()

    dh = 128
    theta = 1.0 / (10000.0 ** (np.arange(0, dh, 2, dtype=np.float64) / dh))
    fr = np.arange(N, dtype=np.float64)[:, None] * theta[None, :]
    scale = dh ** -0.25
    cosin = np.empty((128, N), dtype=np.float32)
    cosin[0:64, :] = (np.cos(fr).T * scale).astype(np.float32)
    cosin[64:128, :] = (np.sin(fr).T * scale).astype(np.float32)
    cosin2 = np.empty((128, N), dtype=np.float32)
    cosin2[0:64, :] = (-np.sin(fr).T * scale).astype(np.float32)
    cosin2[64:128, :] = (np.cos(fr).T * scale).astype(np.float32)

    mask = np.zeros((128, 4 * 512), dtype=np.float32)
    for j in range(4):
        mm, qq = np.meshgrid(np.arange(128), np.arange(512), indexing="ij")
        mask[:, j * 512:(j + 1) * 512] = (qq >= mm + 128 * j).astype(np.float32)
    masksd = mask.astype(BFNP)

    ones2 = np.ones((128, 128), dtype=np.float32)
    iden = np.eye(128, dtype=np.float32)

    wo_rows = []
    for ft in range(16):
        head = 4 * (ft % 4) + ft // 4
        wo_rows.append(W_o[head * 128:(head + 1) * 128, :])
    wo_p = np.ascontiguousarray(np.concatenate(wo_rows, axis=0))

    w1T = np.ascontiguousarray(w1.T)
    w2T = np.ascontiguousarray(w2.T)
    w3T = np.ascontiguousarray(w3.T)

    in_maps = []
    for c in range(8):
        b, qi = c // 4, c % 4
        xb = x[b]
        C_ = N // 512
        xTc = np.ascontiguousarray(
            xb.reshape(C_, 512, 16, 128).transpose(0, 3, 2, 1).reshape(C_ * 128, 16 * 512))
        x_own = np.ascontiguousarray(xb[qi * Nq:(qi + 1) * Nq, :])
        TCq = max(1, Nq // 512)
        x_ownT = np.ascontiguousarray(
            x_own.reshape(TCq, 512, 16, 128).transpose(0, 3, 2, 1).reshape(TCq * 128, 16 * 512))

        cols = []
        for pp in range(2):
            for kind in range(3):
                for hl in range(2):
                    h = 4 * qi + 2 * pp + hl
                    base = kind * (16 * 128)
                    cc = W_qkv[:, base + h * 128: base + (h + 1) * 128]
                    if kind < 2:
                        cc = cc[:, perm]
                    cols.append(cc)
        wqkv_c = np.ascontiguousarray(np.concatenate(cols, axis=1))

        w1s = np.ascontiguousarray(
            w1T[256 * c:256 * (c + 1), :].reshape(2, 128, HT, 128)
            .transpose(1, 2, 0, 3).reshape(128, 2 * HT * 128))
        w2s = np.ascontiguousarray(
            w2T[256 * c:256 * (c + 1), :].reshape(2, 128, HT, 128)
            .transpose(1, 2, 0, 3).reshape(128, 2 * HT * 128))
        w3s = np.ascontiguousarray(
            w3T[:, 256 * c:256 * (c + 1)].reshape(HT, 128, 256)
            .transpose(1, 0, 2).reshape(128, HT * 256))

        P = np.arange(128, dtype=np.int64)[:, None]
        yix = ((np.arange(4)[None, :] * 128 + P) * N + qi * Nq).astype(np.int32)

        in_maps.append(dict(
            xT=xTc, x_own=x_own, x_ownT=x_ownT, wqkv=wqkv_c, wo=wo_p,
            w1s=w1s, w2s=w2s, w3s=w3s, cosin=cosin, masksd=masksd,
            ones2=ones2, iden=iden, yidx=yix, cosin2=cosin2,
        ))
    return in_maps


def kernel(L0_attn_x, L0_attn_W_qkv, L0_attn_W_o, L0_ffn_w1, L0_ffn_w2, L0_ffn_w3):
    x = np.asarray(L0_attn_x, dtype=np.float32)
    N = x.shape[1]
    H = np.asarray(L0_ffn_w1).shape[0]
    Nq = N // 4
    nc = build_nc(N, H)
    in_maps = prep_inputs(
        x, np.asarray(L0_attn_W_qkv, np.float32), np.asarray(L0_attn_W_o, np.float32),
        np.asarray(L0_ffn_w1, np.float32), np.asarray(L0_ffn_w2, np.float32),
        np.asarray(L0_ffn_w3, np.float32), N, H)
    res = run_bass_kernel_spmd(nc, in_maps, core_ids=list(range(8)))
    outp = np.empty((2, N, D), dtype=np.float32)
    for c in range(8):
        b, qi = c // 4, c % 4
        outp[b, qi * Nq:(qi + 1) * Nq, :] = np.asarray(res.results[c]["out"],
                                                       dtype=np.float32)
    return outp
